# version 02
# baseline (speedup 1.0000x reference)
import numpy as np
import ml_dtypes
import concourse.bass as bass
import concourse.mybir as mybir
from concourse.bass_utils import run_bass_kernel_spmd
from contextlib import ExitStack

F32 = mybir.dt.float32
BF16 = mybir.dt.bfloat16
AF = mybir.ActivationFunctionType
ALU = mybir.AluOpType

B, L, D, H = 2, 8192, 1024, 4
DK = 256
CH = 128
SEG = 512
NSEG = L // SEG
NCH = SEG // CH
QT = SEG // 4
MYT = L // 4
PBT = 1024
TW = NCH * CH

C_EPS_LN, C_EPS_L2, C_EPS_RMS, C_NHALF, C_I1024, C_I256, C_ONE, C_ZERO = range(8)

DEBUG_SEG = -1


def build_program():
    nc = bass.Bass("TRN2", target_bir_lowering=False)

    x_in = nc.dram_tensor("x_in", [D, L + 3], F32, kind="ExternalInput").ap()
    x_mine = nc.dram_tensor("x_mine", [D, MYT], F32, kind="ExternalInput").ap()
    wproj = nc.dram_tensor("wproj", [D, 771], F32, kind="ExternalInput").ap()
    convw = nc.dram_tensor("convw", [3 * DK, 4], F32, kind="ExternalInput").ap()
    wot = nc.dram_tensor("wot", [D, D], F32, kind="ExternalInput").ap()
    wct = nc.dram_tensor("wct", [16 * D, D], F32, kind="ExternalInput").ap()
    mk_bd = nc.dram_tensor("mk_bd", [32, 128, 128], F32, kind="ExternalInput").ap()
    up_bd = nc.dram_tensor("up_bd", [32, 128, 512], F32, kind="ExternalInput").ap()
    onormw = nc.dram_tensor("onormw", [128, 2], F32, kind="ExternalInput").ap()
    mixb = nc.dram_tensor("mixb", [1, 1], F32, kind="ExternalInput").ap()
    consts = nc.dram_tensor("consts", [128, 8], F32, kind="ExternalInput").ap()
    msel = nc.dram_tensor("msel", [128, 4], F32, kind="ExternalInput").ap()
    out = nc.dram_tensor("out", [D, MYT], F32, kind="ExternalOutput").ap()

    rowbuf = nc.dram_tensor("rowbuf", [NSEG, 16, SEG], F32).ap()
    rowbuf2 = nc.dram_tensor("rowbuf2", [2, 4, PBT], F32).ap()
    x1_dram = nc.dram_tensor("x1_dram", [D, MYT], F32).ap()
    ln4_dram = nc.dram_tensor("ln4_dram", [D, MYT], BF16).ap()
    cc_in = nc.dram_tensor("cc_in", [2, 128, SEG], BF16).ap()
    cc_out = nc.dram_tensor("cc_out", [4, 2, 128, SEG], BF16).ap()

    dbg = {}
    if DEBUG_SEG >= 0:
        for nm in ("obf", "ocat", "attn", "x1", "G", "vtm", "ktm", "oag", "qkv", "lnx",
                   "ema", "frq", "aT", "emin", "uem", "fg", "ln2f"):
            dbg[nm] = nc.dram_tensor("dbg_" + nm, [128, 8 * SEG], F32, kind="ExternalOutput").ap()

    events = {"tensor": [], "vector": [], "scalar": [], "gpsimd": [], "sync": []}
    ap3 = lambda h, off, dims: bass.AP(h, off, dims)

    with ExitStack() as ctx:
        def sbuf(name, shape, dt=BF16, c=None):
            return (c or ctx).enter_context(nc.sbuf_tensor(name, shape, dt))

        class Chan:
            def __init__(self, name):
                self.sem = ctx.enter_context(nc.semaphore(name))
                self.n = 0

        ch_ld = Chan("ch_ld")
        ch_sld = Chan("ch_sld")
        ch_gld = Chan("ch_gld")
        ch_t = Chan("ch_t")
        ch_v = Chan("ch_v")
        ch_s = Chan("ch_s")
        ch_cc = Chan("ch_cc")
        ch_g = Chan("ch_g")
        CHANNELS = [ch_ld, ch_sld, ch_gld, ch_t, ch_v, ch_s, ch_cc, ch_g]

        def wait(eng_name, *chs):
            if not chs:
                chs = CHANNELS
            snap = [(c.sem, c.n) for c in chs if c.n > 0]

            def fn(eng, snap=snap):
                for sem, n in snap:
                    eng.wait_ge(sem, n)
            events[eng_name].append(fn)

        def _mk(eng_name, ch):
            def emit(fn):
                def g(e, fn=fn):
                    fn(e).then_inc(ch.sem, 1)
                ch.n += 1
                events[eng_name].append(g)
            return emit

        T = _mk("tensor", ch_t)
        V = _mk("vector", ch_v)
        S = _mk("scalar", ch_s)
        G = _mk("gpsimd", ch_g)
        CC = _mk("gpsimd", ch_cc)

        def DMA(eng_name, ch, fn):
            def g(e, fn=fn, ch=ch):
                fn(e).then_inc(ch.sem, 16)
            ch.n += 16
            events[eng_name].append(g)

        def flush_events():
            with nc.Block() as block:
                for name in ("sync", "gpsimd", "scalar", "vector", "tensor"):
                    evs = tuple(events[name])
                    events[name] = []
                    if evs:
                        getattr(block, name)(lambda eng, evs=evs: [fn(eng) for fn in evs])

        s_wproj = sbuf("s_wproj", [128, 8 * 771])
        s_convw = sbuf("s_convw", [128, 24], F32)
        s_compT = sbuf("s_compT", [128, 32 * 128])
        s_onorm = sbuf("s_onorm", [128, 2], F32)
        s_consts = sbuf("s_consts", [128, 8], F32)
        s_msel = sbuf("s_msel", [128, 4], F32)
        s_mixbias = sbuf("s_mixbias", [3, 1], F32)
        s_ones = sbuf("s_ones", [128, 1])
        s_ident = sbuf("s_ident", [128, 128])
        s_identf = sbuf("s_identf", [128, 128], F32)
        s_mLneg = sbuf("s_mLneg", [128, 128])
        s_mUneg = sbuf("s_mUneg", [128, 128])
        s_mUinc = sbuf("s_mUinc", [128, 128])
        s_S = sbuf("s_S", [128, 512], F32)
        s_Sbf = sbuf("s_Sbf", [128, 512])
        s_ema_st = sbuf("s_ema_st", [128, 2], F32)
        s_one1 = sbuf("s_one1", [1, 1], F32)
        s_row_a = sbuf("s_row_a", [1, SEG], F32)
        s_row_b = sbuf("s_row_b", [1, SEG], F32)
        s_row3 = sbuf("s_row3", [3, SEG], F32)
        s_sq = sbuf("s_sq", [128, SEG])

        cst = lambda j: s_consts[:, j:j + 1]

        def prelude(g):
            g.memset(s_ident[:], 1.0)
            g.affine_select(s_ident[:], s_ident[:], [[-1, 128]], ALU.is_equal, 0.0,
                            base=0, channel_multiplier=1)
            g.memset(s_identf[:], 1.0)
            g.affine_select(s_identf[:], s_identf[:], [[-1, 128]], ALU.is_equal, 0.0,
                            base=0, channel_multiplier=1)
            g.memset(s_mLneg[:], -1.0)
            g.affine_select(s_mLneg[:], s_mLneg[:], [[-1, 128]], ALU.is_gt, 0.0,
                            base=0, channel_multiplier=1)
            g.memset(s_mUneg[:], -1.0)
            g.affine_select(s_mUneg[:], s_mUneg[:], [[1, 128]], ALU.is_gt, 0.0,
                            base=0, channel_multiplier=-1)
            g.memset(s_mUinc[:], 1.0)
            g.affine_select(s_mUinc[:], s_mUinc[:], [[1, 128]], ALU.is_ge, 0.0,
                            base=0, channel_multiplier=-1)
            g.memset(s_ones[:], 1.0)
            g.memset(s_one1[:], 1.0)
            g.memset(s_S[:], 0.0)
            g.memset(s_Sbf[:], 0.0)
            g.memset(s_ema_st[:], 0.0)
            return g.memset(s_mixbias[:], 0.0)
        G(prelude)

        DMA("sync", ch_ld, lambda e: e.dma_start(out=s_consts[:], in_=consts))
        DMA("sync", ch_ld, lambda e: e.dma_start(out=s_msel[:], in_=msel))
        DMA("sync", ch_ld, lambda e: e.dma_start(
            out=s_convw[:].rearrange("p (t c) -> p t c", t=6),
            in_=ap3(convw.tensor, 0, [[4, 128], [4 * 128, 6], [1, 4]])))
        DMA("sync", ch_ld, lambda e: e.dma_start(out=s_onorm[:], in_=onormw))
        wait("sync", ch_g)
        DMA("sync", ch_ld, lambda e: e.dma_start(out=s_mixbias[2:3, 0:1], in_=mixb))
        DMA("gpsimd", ch_gld, lambda e: e.dma_start(
            out=s_wproj[:].rearrange("p (t c) -> p t c", t=8),
            in_=ap3(wproj.tensor, 0, [[771, 128], [771 * 128, 8], [1, 771]])))

        with ExitStack() as c2:
            s_mk = sbuf("s_mk", [128, 32 * 128], c=c2)
            s_up = sbuf("s_up", [128, 32 * 512], c=c2)
            ps_A0 = c2.enter_context(nc.psum_tensor("ps_A0", [128, 1024], F32))
            ps_B0 = c2.enter_context(nc.psum_tensor("ps_B0", [128, 1024], F32))
            DMA("gpsimd", ch_gld, lambda e: e.dma_start(
                out=s_mk[:].rearrange("p (g c) -> p g c", g=32),
                in_=ap3(mk_bd.tensor, 0, [[128, 128], [128 * 128, 32], [1, 128]])))
            DMA("gpsimd", ch_gld, lambda e: e.dma_start(
                out=s_up[:].rearrange("p (g c) -> p g c", g=32),
                in_=ap3(up_bd.tensor, 0, [[512, 128], [512 * 128, 32], [1, 512]])))
            wait("tensor", ch_gld, ch_g)
            for g_i in range(32):
                tgt = ps_A0 if (g_i % 2 == 0) else ps_B0
                T(lambda t, g_i=g_i, tgt=tgt: t.matmul(
                    tgt[:, 0:512], s_mk[:, 128 * g_i:128 * (g_i + 1)],
                    s_up[:, 512 * g_i:512 * (g_i + 1)], start=True, stop=True))
                wait("vector", ch_t)
                for j in range(4):
                    V(lambda v, g_i=g_i, j=j, tgt=tgt: v.tensor_copy(
                        s_compT[32 * j:32 * (j + 1), 128 * g_i:128 * (g_i + 1)],
                        tgt[32 * j:32 * (j + 1), 128 * j:128 * (j + 1)]))
                if g_i % 2 == 1:
                    wait("tensor", ch_v)
            wait("tensor", ch_v)
            wait("sync", ch_v)
            wait("scalar", ch_v)
            wait("gpsimd", ch_v)
            flush_events()

        with ExitStack() as c1:
            s_wot = sbuf("s_wot", [128, 8 * 1024], c=c1)
            DMA("gpsimd", ch_gld, lambda e: e.dma_start(
                out=s_wot[:].rearrange("p (t c) -> p t c", t=8),
                in_=ap3(wot.tensor, 0, [[1024, 128], [1024 * 128, 8], [1, 1024]])))

            s_xs = sbuf("s_xs", [128, 8 * SEG], c=c1)
            s_lnx = sbuf("s_lnx", [128, 8 * SEG], c=c1)
            s_raw = sbuf("s_raw", [128, 6 * (SEG + 3)], c=c1)
            s_qkv = sbuf("s_qkv", [128, 6 * SEG], c=c1)
            s_khat = sbuf("s_khat", [128, 2 * SEG], c=c1)
            s_kb = sbuf("s_kb", [128, 2 * SEG], c=c1)
            s_vb = sbuf("s_vb", [128, 2 * SEG], c=c1)
            s_ktm = sbuf("s_ktm", [128, NCH * 256], c=c1)
            s_vtm = sbuf("s_vtm", [128, NCH * 256], c=c1)
            s_f_m1 = sbuf("s_f_m1", [128, SEG], F32, c=c1)
            s_f_r1 = sbuf("s_f_r1", [128, SEG], F32, c=c1)
            s_f_b = sbuf("s_f_b", [128, SEG], F32, c=c1)
            s_f_g = sbuf("s_f_g", [128, SEG], F32, c=c1)
            s_f_g1m = sbuf("s_f_g1m", [128, SEG], F32, c=c1)
            s_f_mix = sbuf("s_f_mix", [128, SEG], F32, c=c1)
            s_f_rq = sbuf("s_f_rq", [128, SEG], F32, c=c1)
            s_f_rk = sbuf("s_f_rk", [128, SEG], F32, c=c1)
            s_f_rms = sbuf("s_f_rms", [128, SEG], F32, c=c1)
            s_f_tmp = sbuf("s_f_tmp", [128, SEG], F32, c=c1)
            s_A = sbuf("s_A", [128, TW], c=c1)
            s_AT = sbuf("s_AT", [128, TW], c=c1)
            s_X = [sbuf(f"s_X{k}", [128, TW], c=c1) for k in range(6)]
            s_XT = [sbuf(f"s_XT{k}", [128, TW], c=c1) for k in range(2)]
            s_G = sbuf("s_G", [128, TW], c=c1)
            s_aT = sbuf("s_aT", [128, TW], c=c1)
            s_r = sbuf("s_r", [128, 256], c=c1)
            s_u = sbuf("s_u", [128, 256], c=c1)
            s_uema = sbuf("s_uema", [128, 2 * SEG], F32, c=c1)
            s_lcA = sbuf("s_lcA", [1, SEG], F32, c=c1)
            s_lcB = sbuf("s_lcB", [1, SEG], F32, c=c1)
            s_neglc = sbuf("s_neglc", [128, 4], F32, c=c1)
            s_f_lc = sbuf("s_f_lc", [128, SEG], F32, c=c1)
            s_f_p = sbuf("s_f_p", [128, SEG], F32, c=c1)
            s_uetm = sbuf("s_uetm", [128, NCH * 256], F32, c=c1)
            s_mt = sbuf("s_mt", [128, 128], F32, c=c1)
            s_mtf = sbuf("s_mtf", [128, NCH * 128], F32, c=c1)
            s_emtmp = sbuf("s_emtmp", [128, 128], F32, c=c1)
            s_ema = sbuf("s_ema", [128, 2 * SEG], F32, c=c1)
            s_oseg = sbuf("s_oseg", [128, 2 * SEG], F32, c=c1)
            s_obf = sbuf("s_obf", [128, 2 * SEG], c=c1)
            s_oag = sbuf("s_oag", [128, 8 * SEG], c=c1)
            s_ocat = sbuf("s_ocat", [128, 8 * QT], c=c1)
            s_attn = sbuf("s_attn", [128, 8 * QT], F32, c=c1)
            s_abf = sbuf("s_abf", [128, 8 * QT], c=c1)
            s_x1 = sbuf("s_x1", [128, 8 * QT], F32, c=c1)
            s_xres = sbuf("s_xres", [128, 8 * QT], F32, c=c1)
            s_fq_a = sbuf("s_fq_a", [128, QT], F32, c=c1)
            s_fq_b = sbuf("s_fq_b", [128, QT], F32, c=c1)
            s_fq_c = sbuf("s_fq_c", [128, QT], F32, c=c1)

            ps_A = c1.enter_context(nc.psum_tensor("ps_A", [128, 1024], F32))
            ps_B = c1.enter_context(nc.psum_tensor("ps_B", [128, 1024], F32))
            ps_C = c1.enter_context(nc.psum_tensor("ps_C", [128, 512], F32))
            ps_D = c1.enter_context(nc.psum_tensor("ps_D", [128, 512], F32))
            ps_E = c1.enter_context(nc.psum_tensor("ps_E", [128, 512], F32))
            ps_F = c1.enter_context(nc.psum_tensor("ps_F", [128, 512], F32))

            def row_store(seg, r0, src, nr=1, w=SEG):
                DMA("scalar", ch_sld, lambda e, seg=seg, r0=r0, src=src, nr=nr, w=w: e.dma_start(
                    out=ap3(rowbuf.tensor, (seg * 16 + r0) * SEG, [[SEG, nr], [1, w]]),
                    in_=src[0:nr, 0:w]))

            def row_bcast(seg, row, dst, w=SEG):
                DMA("scalar", ch_sld, lambda e, seg=seg, row=row, dst=dst, w=w: e.dma_start(
                    out=dst[:, 0:w],
                    in_=ap3(rowbuf.tensor, (seg * 16 + row) * SEG, [[0, 128], [1, w]])))

            def ln_fields(mf, vf, tmp, w=SEG):
                wait("vector", ch_sld)
                V(lambda v: v.tensor_scalar(mf[:, 0:w], mf[:, 0:w], cst(C_I1024), None, ALU.mult))
                V(lambda v: v.tensor_scalar(vf[:, 0:w], vf[:, 0:w], cst(C_I1024), None, ALU.mult))
                wait("scalar", ch_v)
                S(lambda s: s.activation(tmp[:, 0:w], mf[:, 0:w], AF.Square))
                wait("vector", ch_s)
                V(lambda v: v.tensor_sub(vf[:, 0:w], vf[:, 0:w], tmp[:, 0:w]))
                wait("scalar", ch_v)
                S(lambda s: s.activation(vf[:, 0:w], vf[:, 0:w], AF.Ln, bias=cst(C_EPS_LN)))
                S(lambda s: s.activation(vf[:, 0:w], vf[:, 0:w], AF.Exp, scale=cst(C_NHALF)))

            for seg in range(NSEG):
                c0 = seg * SEG

                wait("gpsimd", ch_t, ch_s, ch_v)
                wait("tensor", ch_v, ch_s, ch_sld, ch_ld, ch_gld)
                wait("vector", ch_t, ch_s, ch_sld, ch_ld, ch_gld)
                wait("scalar", ch_t, ch_v, ch_sld, ch_ld, ch_gld)
                DMA("gpsimd", ch_gld, lambda e, c0=c0: e.dma_start(
                    out=s_xs[:].rearrange("p (t c) -> p t c", t=8),
                    in_=ap3(x_in.tensor, 3 + c0, [[L + 3, 128], [(L + 3) * 128, 8], [1, SEG]])))

                wait("tensor", ch_gld, ch_ld, ch_g)
                wait("scalar", ch_gld)
                for dt_i in range(8):
                    S(lambda s, dt_i=dt_i: s.activation(
                        s_sq[:], s_xs[:, SEG * dt_i:SEG * (dt_i + 1)], AF.Square))
                    wait("tensor", ch_s)
                    T(lambda t, dt_i=dt_i: t.matmul(ps_F[0:1, 0:SEG], s_ones[:], s_sq[:],
                                                    start=(dt_i == 0), stop=(dt_i == 7)))
                    if dt_i < 7:
                        wait("scalar", ch_t)
                for dt_i in range(8):
                    T(lambda t, dt_i=dt_i: t.matmul(ps_E[0:1, 0:SEG], s_ones[:],
                                                    s_xs[:, SEG * dt_i:SEG * (dt_i + 1)],
                                                    start=(dt_i == 0), stop=(dt_i == 7)))
                wait("vector", ch_t, ch_sld)
                V(lambda v: v.tensor_copy(s_row_a[0:1, :], ps_E[0:1, 0:SEG]))
                V(lambda v: v.tensor_copy(s_row_b[0:1, :], ps_F[0:1, 0:SEG]))
                wait("scalar", ch_v)
                row_store(seg, 0, s_row_a)
                row_store(seg, 1, s_row_b)
                wait("scalar", ch_sld)
                row_bcast(seg, 0, s_f_m1)
                row_bcast(seg, 1, s_f_r1)
                ln_fields(s_f_m1, s_f_r1, s_f_tmp)
                wait("vector", ch_s)
                for dt_i in range(8):
                    V(lambda v, dt_i=dt_i: v.tensor_sub(
                        s_lnx[:, SEG * dt_i:SEG * (dt_i + 1)],
                        s_xs[:, SEG * dt_i:SEG * (dt_i + 1)], s_f_m1[:]))
                    V(lambda v, dt_i=dt_i: v.tensor_mul(
                        s_lnx[:, SEG * dt_i:SEG * (dt_i + 1)],
                        s_lnx[:, SEG * dt_i:SEG * (dt_i + 1)], s_f_r1[:]))

                if seg > 0:
                    V(lambda v: v.tensor_copy(
                        s_raw[:].rearrange("p (t c) -> p t c", t=6)[:, :, 0:3],
                        s_raw[:].rearrange("p (t c) -> p t c", t=6)[:, :, SEG:SEG + 3]))

                wait("tensor", ch_v)
                for ot in range(6):
                    tgt = ps_A if (ot % 2 == 0) else ps_B
                    for kt in range(8):
                        T(lambda t, ot=ot, kt=kt, tgt=tgt: t.matmul(
                            tgt[:, 0:SEG],
                            s_wproj[:, 771 * kt + 128 * ot:771 * kt + 128 * (ot + 1)],
                            s_lnx[:, SEG * kt:SEG * (kt + 1)],
                            start=(kt == 0), stop=(kt == 7)))
                    wait("vector", ch_t)
                    V(lambda v, ot=ot, tgt=tgt: v.tensor_copy(
                        s_raw[:, (SEG + 3) * ot + 3:(SEG + 3) * (ot + 1)], tgt[:, 0:SEG]))
                    if ot % 2 == 1:
                        wait("tensor", ch_v)
                for kt in range(8):
                    T(lambda t, kt=kt: t.matmul(
                        ps_E[0:3, 0:SEG], s_wproj[:, 771 * kt + 768:771 * kt + 771],
                        s_lnx[:, SEG * kt:SEG * (kt + 1)], start=(kt == 0), stop=(kt == 7)))
                wait("scalar", ch_t, ch_sld)
                S(lambda s: s.activation(s_row3[0:3, :], ps_E[0:3, 0:SEG], AF.Sigmoid,
                                         bias=s_mixbias[:]))
                wait("scalar", ch_s)
                row_store(seg, 2, s_row3, nr=3)
                wait("scalar", ch_sld)
                row_bcast(seg, 2, s_f_b)
                row_bcast(seg, 3, s_f_g)
                row_bcast(seg, 4, s_f_mix)
                DMA("scalar", ch_sld, lambda e, seg=seg: e.dma_start(
                    out=s_lcA[0:1, :],
                    in_=ap3(rowbuf.tensor, (seg * 16 + 3) * SEG, [[SEG, 1], [1, SEG]])))
                wait("scalar", ch_sld)
                S(lambda s: s.activation(s_lcA[0:1, :], s_lcA[0:1, :], AF.Ln))
                wait("vector", ch_s)
                V(lambda v: v.tensor_scalar_max(s_lcA[0:1, :], s_lcA[0:1, :], -80.0))
                cur, oth = s_lcA, s_lcB
                sh = 1
                while sh < CH:
                    V(lambda v, cur=cur, oth=oth, sh=sh: v.tensor_copy(
                        ap3(oth, 0, [[SEG, 1], [CH, NCH], [1, sh]]),
                        ap3(cur, 0, [[SEG, 1], [CH, NCH], [1, sh]])))
                    V(lambda v, cur=cur, oth=oth, sh=sh: v.tensor_add(
                        ap3(oth, sh, [[SEG, 1], [CH, NCH], [1, CH - sh]]),
                        ap3(cur, sh, [[SEG, 1], [CH, NCH], [1, CH - sh]]),
                        ap3(cur, 0, [[SEG, 1], [CH, NCH], [1, CH - sh]])))
                    cur, oth = oth, cur
                    sh *= 2
                s_lc = cur
                wait("tensor", ch_v)
                for c in range(NCH):
                    T(lambda t, c=c, s_lc=s_lc: t.matmul(
                        ps_F[:, c:c + 1], s_lc[0:1, 128 * c:128 * (c + 1)], s_one1[0:1, 0:1],
                        start=True, stop=True))
                wait("vector", ch_t)
                V(lambda v: v.tensor_scalar(s_neglc[:], ps_F[:, 0:4], -1.0, None, ALU.mult))
                wait("tensor", ch_v)
                wait("scalar", ch_v)
                row_store(seg, 12, s_lc)
                wait("scalar", ch_sld)
                row_bcast(seg, 12, s_f_lc)

                wait("vector", ch_t)
                for ot in range(6):
                    rb = (SEG + 3) * ot
                    V(lambda v, ot=ot, rb=rb: v.tensor_scalar(
                        s_qkv[:, SEG * ot:SEG * (ot + 1)], s_raw[:, rb:rb + SEG],
                        s_convw[:, 4 * ot:4 * ot + 1], None, ALU.mult))
                    for j in (1, 2, 3):
                        V(lambda v, ot=ot, rb=rb, j=j: v.scalar_tensor_tensor(
                            s_qkv[:, SEG * ot:SEG * (ot + 1)],
                            s_raw[:, rb + j:rb + j + SEG],
                            s_convw[:, 4 * ot + j:4 * ot + j + 1],
                            s_qkv[:, SEG * ot:SEG * (ot + 1)], ALU.mult, ALU.add))
                    wait("scalar", ch_v)
                    S(lambda s, ot=ot: s.activation(
                        s_qkv[:, SEG * ot:SEG * (ot + 1)],
                        s_qkv[:, SEG * ot:SEG * (ot + 1)], AF.Silu))

                wait("scalar", ch_s)
                for base, row in ((0, 5), (2, 6)):
                    for i in range(2):
                        S(lambda s, base=base, i=i: s.activation(
                            s_sq[:], s_qkv[:, SEG * (base + i):SEG * (base + i + 1)], AF.Square))
                        wait("tensor", ch_s)
                        T(lambda t, i=i: t.matmul(ps_F[0:1, 0:SEG], s_ones[:], s_sq[:],
                                                  start=(i == 0), stop=(i == 1)))
                        wait("scalar", ch_t)
                    dst = s_row_a if row == 5 else s_row_b
                    wait("vector", ch_t, ch_sld)
                    V(lambda v, dst=dst: v.tensor_copy(dst[0:1, :], ps_F[0:1, 0:SEG]))
                    wait("tensor", ch_v)
                wait("scalar", ch_v)
                row_store(seg, 5, s_row_a)
                row_store(seg, 6, s_row_b)
                wait("scalar", ch_sld)
                row_bcast(seg, 5, s_f_rq)
                row_bcast(seg, 6, s_f_rk)
                wait("scalar", ch_sld)
                for f in (s_f_rq, s_f_rk):
                    S(lambda s, f=f: s.activation(f[:], f[:], AF.Ln, bias=cst(C_EPS_L2)))
                    S(lambda s, f=f: s.activation(f[:], f[:], AF.Exp, scale=cst(C_NHALF)))

                wait("vector", ch_s, ch_ld, ch_sld)
                for i in range(2):
                    V(lambda v, i=i: v.tensor_mul(
                        s_khat[:, SEG * i:SEG * (i + 1)],
                        s_qkv[:, SEG * (2 + i):SEG * (3 + i)], s_f_rk[:]))
                    V(lambda v, i=i: v.tensor_mul(
                        s_kb[:, SEG * i:SEG * (i + 1)],
                        s_khat[:, SEG * i:SEG * (i + 1)], s_f_b[:]))
                    V(lambda v, i=i: v.tensor_mul(
                        s_vb[:, SEG * i:SEG * (i + 1)],
                        s_qkv[:, SEG * (4 + i):SEG * (5 + i)], s_f_b[:]))

                wait("sync", ch_v, ch_t)
                for (src, dst) in ((s_khat, s_ktm), (s_vb, s_vtm)):
                    for ft in range(2):
                        def tr(e, src=src, dst=dst, ft=ft):
                            o3 = ap3(dst, 128 * ft, [[NCH * 256, 128], [256, NCH], [1, 128]])
                            i3 = src[:, SEG * ft:SEG * (ft + 1)].rearrange("p (c t) -> p c t", c=NCH)
                            return e.dma_start_transpose(o3, i3)
                        DMA("sync", ch_ld, tr)

                if seg == DEBUG_SEG and dbg:
                    V(lambda v: v.tensor_copy(s_f_tmp[:, 0:2], s_ema_st[:]))
                V(lambda v: v.tensor_scalar(s_f_g1m[:], s_f_g[:], -1.0, 1.0, ALU.mult, ALU.add))
                for i in range(2):
                    V(lambda v, i=i: v.tensor_mul(
                        s_uema[:, SEG * i:SEG * (i + 1)],
                        s_qkv[:, SEG * (4 + i):SEG * (5 + i)], s_f_g1m[:]))
                wait("tensor", ch_v)
                for ft in range(2):
                    for c in range(NCH):
                        T(lambda t, ft=ft, c=c: t.transpose(
                            ps_A[:, 512 * ft + 128 * c:512 * ft + 128 * (c + 1)],
                            s_uema[:, SEG * ft + 128 * c:SEG * ft + 128 * (c + 1)],
                            s_identf[:]))
                wait("vector", ch_t)
                def ev_uetm(v):
                    o3 = ap3(s_uetm, 0, [[NCH * 256, 128], [128, 2 * NCH], [1, 128]])
                    i3 = ap3(ps_A, 0, [[1024, 128], [128, 2 * NCH], [1, 128]])
                    return v.tensor_copy(o3, i3)
                events["vector"].pop() if False else None
                def ev_uetm2(v):
                    for_cols = None
                    return v.tensor_copy(
                        ap3(s_uetm, 0, [[NCH * 256, 128], [1, 1024]]),
                        ap3(ps_A, 0, [[1024, 128], [1, 1024]]))
                V(ev_uetm2)
                wait("scalar", ch_v, ch_sld)
                S(lambda s: s.activation(s_f_p[:], s_f_lc[:], AF.Exp))
                for c in range(NCH):
                    wait("vector", ch_s)
                    V(lambda v, c=c: v.tensor_scalar(s_mt[:], s_f_lc[:, 128 * c:128 * (c + 1)],
                                                     s_neglc[:, c:c + 1], 0.0,
                                                     ALU.add, ALU.min))
                    wait("scalar", ch_v)
                    S(lambda s, c=c: s.activation(s_mt[:], s_mt[:], AF.Exp))
                    wait("vector", ch_s)
                    V(lambda v, c=c: v.tensor_tensor(s_mtf[:, 128 * c:128 * (c + 1)],
                                                     s_mt[:], s_mUinc[:], ALU.mult))
                wait("tensor", ch_v)
                for c in range(NCH):
                    for dv in range(2):
                        T(lambda t, c=c, dv=dv: t.matmul(
                            ps_B[:, 256 * c + 128 * dv:256 * c + 128 * (dv + 1)],
                            s_uetm[:, 512 * dv + 128 * c:512 * dv + 128 * (c + 1)],
                            s_mtf[:, 128 * c:128 * (c + 1)], start=True, stop=True))
                wait("vector", ch_t)
                for c in range(NCH):
                    for dv in range(2):
                        def ema_t1(v, c=c, dv=dv):
                            cb = ap3(s_ema_st, dv, [[2, 128], [0, 128]])
                            return v.tensor_tensor(s_emtmp[:],
                                                   s_f_p[:, 128 * c:128 * (c + 1)], cb, ALU.mult)
                        V(ema_t1)
                        def ema_t2(v, c=c, dv=dv):
                            return v.tensor_add(
                                s_ema[:, SEG * dv + 128 * c:SEG * dv + 128 * (c + 1)],
                                s_emtmp[:],
                                ps_B[:, 256 * c + 128 * dv:256 * c + 128 * (dv + 1)])
                        V(ema_t2)
                    for dv in range(2):
                        V(lambda v, c=c, dv=dv: v.tensor_copy(
                            s_ema_st[:, dv:dv + 1],
                            s_ema[:, SEG * dv + 128 * (c + 1) - 1:SEG * dv + 128 * (c + 1)]))
                wait("tensor", ch_v)

                wait("tensor", ch_ld, ch_v)
                for c in range(NCH):
                    for ki in range(2):
                        T(lambda t, c=c, ki=ki: t.matmul(
                            ps_A[:, 128 * c:128 * (c + 1)],
                            s_kb[:, SEG * ki + 128 * c:SEG * ki + 128 * (c + 1)],
                            s_khat[:, SEG * ki + 128 * c:SEG * ki + 128 * (c + 1)],
                            start=(ki == 0), stop=(ki == 1)))
                    for ki in range(2):
                        T(lambda t, c=c, ki=ki: t.matmul(
                            ps_B[:, 128 * c:128 * (c + 1)],
                            s_khat[:, SEG * ki + 128 * c:SEG * ki + 128 * (c + 1)],
                            s_kb[:, SEG * ki + 128 * c:SEG * ki + 128 * (c + 1)],
                            start=(ki == 0), stop=(ki == 1)))
                wait("vector", ch_t)
                mask_bc = lambda m: ap3(m, 0, [[128, 128], [0, NCH], [1, 128]])
                psv = lambda ps: ps[:, 0:TW].rearrange("p (c t) -> p c t", c=NCH)
                sbv = lambda s: s[:].rearrange("p (c t) -> p c t", c=NCH)
                V(lambda v: v.tensor_tensor(sbv(s_A), psv(ps_A), mask_bc(s_mLneg), ALU.mult))
                V(lambda v: v.tensor_tensor(sbv(s_AT), psv(ps_B), mask_bc(s_mUneg), ALU.mult))
                wait("tensor", ch_v)
                for c in range(NCH):
                    for ki in range(2):
                        T(lambda t, c=c, ki=ki: t.matmul(
                            ps_A[:, 128 * c:128 * (c + 1)],
                            s_khat[:, SEG * ki + 128 * c:SEG * ki + 128 * (c + 1)],
                            s_qkv[:, SEG * ki + 128 * c:SEG * ki + 128 * (c + 1)],
                            start=(ki == 0), stop=(ki == 1)))
                wait("vector", ch_t)
                V(lambda v: v.tensor_tensor(sbv(s_aT), psv(ps_A), mask_bc(s_mUinc), ALU.mult))
                wait("tensor", ch_v)

                curX, curXT = s_A, s_AT
                for lvl in range(6):
                    dstX = s_X[lvl]
                    dstXT = s_XT[lvl % 2] if lvl < 5 else None
                    for c in range(NCH):
                        T(lambda t, c=c, curX=curX, curXT=curXT: t.matmul(
                            ps_A[:, 128 * c:128 * (c + 1)],
                            curXT[:, 128 * c:128 * (c + 1)],
                            curX[:, 128 * c:128 * (c + 1)], start=True, stop=True))
                    if dstXT is not None:
                        for c in range(NCH):
                            T(lambda t, c=c, curX=curX, curXT=curXT: t.matmul(
                                ps_B[:, 128 * c:128 * (c + 1)],
                                curX[:, 128 * c:128 * (c + 1)],
                                curXT[:, 128 * c:128 * (c + 1)], start=True, stop=True))
                    wait("vector", ch_t)
                    V(lambda v, dstX=dstX: v.tensor_copy(dstX[:], ps_A[:, 0:TW]))
                    if dstXT is not None:
                        V(lambda v, dstXT=dstXT: v.tensor_copy(dstXT[:], ps_B[:, 0:TW]))
                    wait("tensor", ch_v)
                    curX, curXT = dstX, dstXT
                V(lambda v: v.tensor_tensor(sbv(s_G), sbv(s_AT),
                                            ap3(s_ident, 0, [[128, 128], [0, NCH], [1, 128]]),
                                            ALU.add))
                wait("tensor", ch_v)
                for lvl in range(6):
                    Xk = s_X[lvl]
                    for c in range(NCH):
                        T(lambda t, c=c, Xk=Xk: t.matmul(
                            ps_A[:, 128 * c:128 * (c + 1)], Xk[:, 128 * c:128 * (c + 1)],
                            s_G[:, 128 * c:128 * (c + 1)], start=True, stop=False))
                        T(lambda t, c=c: t.matmul(
                            ps_A[:, 128 * c:128 * (c + 1)], s_ident[:],
                            s_G[:, 128 * c:128 * (c + 1)], start=False, stop=True))
                    wait("vector", ch_t)
                    V(lambda v: v.tensor_copy(s_G[:], ps_A[:, 0:TW]))
                    wait("tensor", ch_v)

                for c in range(NCH):
                    T(lambda t, c=c: t.matmul(ps_C[:, 0:256], s_kb[:, 128 * c:128 * (c + 1)],
                                              s_Sbf[:, 0:256], start=True, stop=False))
                    T(lambda t, c=c: t.matmul(ps_C[:, 0:256],
                                              s_kb[:, SEG + 128 * c:SEG + 128 * (c + 1)],
                                              s_Sbf[:, 256:512], start=False, stop=True))
                    wait("vector", ch_t)
                    V(lambda v, c=c: v.tensor_sub(s_r[:], s_vtm[:, 256 * c:256 * (c + 1)],
                                                  ps_C[:, 0:256]))
                    wait("tensor", ch_v)
                    T(lambda t, c=c: t.matmul(ps_D[:, 0:256], s_G[:, 128 * c:128 * (c + 1)],
                                              s_r[:], start=True, stop=True))
                    wait("scalar", ch_t)
                    S(lambda s: s.copy(s_u[:], ps_D[:, 0:256]))
                    wait("tensor", ch_s)
                    for dv in range(2):
                        T(lambda t, c=c, dv=dv: t.matmul(
                            ps_D[:, 256 + 128 * dv:384 + 128 * dv],
                            s_Sbf[:, 128 * dv:128 * (dv + 1)],
                            s_qkv[:, 128 * c:128 * (c + 1)], start=True, stop=False))
                        T(lambda t, c=c, dv=dv: t.matmul(
                            ps_D[:, 256 + 128 * dv:384 + 128 * dv],
                            s_Sbf[:, 256 + 128 * dv:384 + 128 * dv],
                            s_qkv[:, SEG + 128 * c:SEG + 128 * (c + 1)],
                            start=False, stop=False))
                        T(lambda t, c=c, dv=dv: t.matmul(
                            ps_D[:, 256 + 128 * dv:384 + 128 * dv],
                            s_u[:, 128 * dv:128 * (dv + 1)],
                            s_aT[:, 128 * c:128 * (c + 1)], start=False, stop=True))
                    wait("vector", ch_t)
                    def ev_o(v, c=c):
                        o_ap = ap3(s_oseg, 128 * c, [[2 * SEG, 128], [SEG, 2], [1, 128]])
                        ps_view = ap3(ps_D, 256, [[512, 128], [128, 2], [1, 128]])
                        rq3 = ap3(s_f_rq, 128 * c, [[SEG, 128], [0, 2], [1, 128]])
                        return v.tensor_tensor(o_ap, ps_view, rq3, ALU.mult)
                    V(ev_o)
                    T(lambda t, c=c: t.matmul(ps_C[:, 0:256], s_ktm[:, 256 * c:256 * c + 128],
                                              s_u[:], start=True, stop=True))
                    T(lambda t, c=c: t.matmul(ps_C[:, 256:512],
                                              s_ktm[:, 256 * c + 128:256 * (c + 1)],
                                              s_u[:], start=True, stop=True))
                    wait("vector", ch_t)
                    V(lambda v: v.tensor_add(s_S[:], s_S[:], ps_C[:]))
                    wait("scalar", ch_v)
                    S(lambda s: s.copy(s_Sbf[:], s_S[:]))
                    wait("tensor", ch_s)

                for i in range(2):
                    V(lambda v, i=i: v.tensor_sub(s_uema[:, SEG * i:SEG * (i + 1)],
                                                  s_ema[:, SEG * i:SEG * (i + 1)],
                                                  s_oseg[:, SEG * i:SEG * (i + 1)]))
                    V(lambda v, i=i: v.tensor_mul(s_uema[:, SEG * i:SEG * (i + 1)],
                                                  s_uema[:, SEG * i:SEG * (i + 1)], s_f_mix[:]))
                    V(lambda v, i=i: v.tensor_add(s_oseg[:, SEG * i:SEG * (i + 1)],
                                                  s_oseg[:, SEG * i:SEG * (i + 1)],
                                                  s_uema[:, SEG * i:SEG * (i + 1)]))
                wait("scalar", ch_v)
                for i in range(2):
                    S(lambda s, i=i: s.activation(s_sq[:], s_oseg[:, SEG * i:SEG * (i + 1)],
                                                  AF.Square))
                    wait("tensor", ch_s)
                    T(lambda t, i=i: t.matmul(ps_F[0:1, 0:SEG], s_ones[:], s_sq[:],
                                              start=(i == 0), stop=(i == 1)))
                    wait("scalar", ch_t)
                wait("vector", ch_t, ch_sld)
                V(lambda v: v.tensor_copy(s_row_a[0:1, :], ps_F[0:1, 0:SEG]))
                wait("scalar", ch_v)
                row_store(seg, 7, s_row_a)
                wait("scalar", ch_sld)
                row_bcast(seg, 7, s_f_rms)
                wait("scalar", ch_sld)
                S(lambda s: s.activation(s_f_rms[:], s_f_rms[:], AF.Ln,
                                         bias=cst(C_EPS_RMS), scale=cst(C_I256)))
                S(lambda s: s.activation(s_f_rms[:], s_f_rms[:], AF.Exp, scale=cst(C_NHALF)))
                wait("vector", ch_s)
                for i in range(2):
                    V(lambda v, i=i: v.tensor_mul(s_oseg[:, SEG * i:SEG * (i + 1)],
                                                  s_oseg[:, SEG * i:SEG * (i + 1)], s_f_rms[:]))
                    V(lambda v, i=i: v.tensor_scalar(s_obf[:, SEG * i:SEG * (i + 1)],
                                                     s_oseg[:, SEG * i:SEG * (i + 1)],
                                                     s_onorm[:, i:i + 1], None, ALU.mult))

                wait("gpsimd", ch_v)
                DMA("gpsimd", ch_gld, lambda e: e.dma_start(
                    out=ap3(cc_in.tensor, 0, [[SEG, 128], [128 * SEG, 2], [1, SEG]]),
                    in_=s_obf[:].rearrange("p (d t) -> p d t", d=2)))
                wait("gpsimd", ch_gld)
                CC(lambda g: g.collective_compute(
                    "AllGather", ALU.bypass,
                    replica_groups=[[0, 1, 2, 3], [4, 5, 6, 7]],
                    ins=[cc_in.opt()], outs=[cc_out.opt()]))
                wait("gpsimd", ch_cc)
                DMA("gpsimd", ch_gld, lambda e: e.dma_start(
                    out=s_oag[:].rearrange("p (f t) -> p f t", f=8),
                    in_=ap3(cc_out.tensor, 0, [[SEG, 128], [128 * SEG, 8], [1, SEG]])))

                wait("vector", ch_gld, ch_ld)
                V(lambda v: v.tensor_scalar(
                    s_ocat[:].rearrange("p (f t) -> p f t", f=8),
                    ap3(s_oag, 0, [[8 * SEG, 128], [SEG, 8], [1, QT]]),
                    s_msel[:, 0:1], None, ALU.mult))
                for p in range(1, 4):
                    V(lambda v, p=p: v.scalar_tensor_tensor(
                        s_ocat[:].rearrange("p (f t) -> p f t", f=8),
                        ap3(s_oag, QT * p, [[8 * SEG, 128], [SEG, 8], [1, QT]]),
                        s_msel[:, p:p + 1],
                        s_ocat[:].rearrange("p (f t) -> p f t", f=8),
                        ALU.mult, ALU.add))
                wait("tensor", ch_v)
                for ot in range(8):
                    tgt = ps_E if (ot % 2 == 0) else ps_F
                    for kt in range(8):
                        T(lambda t, ot=ot, kt=kt, tgt=tgt: t.matmul(
                            tgt[:, 0:QT],
                            s_wot[:, 1024 * kt + 128 * ot:1024 * kt + 128 * (ot + 1)],
                            s_ocat[:, QT * kt:QT * (kt + 1)],
                            start=(kt == 0), stop=(kt == 7)))
                    wait("vector", ch_t)
                    V(lambda v, ot=ot, tgt=tgt: v.tensor_copy(
                        s_attn[:, QT * ot:QT * (ot + 1)], tgt[:, 0:QT]))
                    if ot % 2 == 1:
                        wait("tensor", ch_v)
                V(lambda v: v.tensor_copy(s_abf[:], s_attn[:]))
                wait("tensor", ch_v)
                for ot in range(8):
                    T(lambda t, ot=ot: t.matmul(ps_E[0:1, 0:QT], s_ones[:],
                                                s_abf[:, QT * ot:QT * (ot + 1)],
                                                start=(ot == 0), stop=(ot == 7)))
                wait("scalar", ch_v)
                for ot in range(8):
                    S(lambda s, ot=ot: s.activation(s_sq[:, 0:QT],
                                                    s_abf[:, QT * ot:QT * (ot + 1)], AF.Square))
                    wait("tensor", ch_s)
                    T(lambda t, ot=ot: t.matmul(ps_F[0:1, 0:QT], s_ones[:], s_sq[:, 0:QT],
                                                start=(ot == 0), stop=(ot == 7)))
                    wait("scalar", ch_t)
                wait("vector", ch_t, ch_sld)
                V(lambda v: v.tensor_copy(s_row_a[0:1, 0:QT], ps_E[0:1, 0:QT]))
                V(lambda v: v.tensor_copy(s_row_b[0:1, 0:QT], ps_F[0:1, 0:QT]))
                wait("scalar", ch_v)
                row_store(seg, 8, s_row_a, w=QT)
                row_store(seg, 9, s_row_b, w=QT)
                wait("scalar", ch_sld)
                row_bcast(seg, 8, s_fq_a, w=QT)
                row_bcast(seg, 9, s_fq_b, w=QT)
                ln_fields(s_fq_a, s_fq_b, s_fq_c, w=QT)
                if seg == DEBUG_SEG and dbg:
                    wait("vector", ch_s)
                    V(lambda v: v.tensor_copy(s_f_g1m[:, 0:QT], s_fq_a[:, 0:QT]))
                    V(lambda v: v.tensor_copy(s_f_g1m[:, QT:2 * QT], s_fq_b[:, 0:QT]))
                DMA("sync", ch_ld, lambda e, seg=seg: e.dma_start(
                    out=s_xres[:].rearrange("p (f t) -> p f t", f=8),
                    in_=ap3(x_mine.tensor, QT * seg, [[MYT, 128], [MYT * 128, 8], [1, QT]])))
                wait("vector", ch_s, ch_ld)
                for ot in range(8):
                    V(lambda v, ot=ot: v.tensor_sub(s_x1[:, QT * ot:QT * (ot + 1)],
                                                    s_attn[:, QT * ot:QT * (ot + 1)], s_fq_a[:]))
                    V(lambda v, ot=ot: v.tensor_mul(s_x1[:, QT * ot:QT * (ot + 1)],
                                                    s_x1[:, QT * ot:QT * (ot + 1)], s_fq_b[:]))
                    V(lambda v, ot=ot: v.tensor_add(s_x1[:, QT * ot:QT * (ot + 1)],
                                                    s_x1[:, QT * ot:QT * (ot + 1)],
                                                    s_xres[:, QT * ot:QT * (ot + 1)]))
                wait("sync", ch_v)
                DMA("sync", ch_ld, lambda e, seg=seg: e.dma_start(
                    out=ap3(x1_dram.tensor, QT * seg, [[MYT, 128], [MYT * 128, 8], [1, QT]]),
                    in_=s_x1[:].rearrange("p (f t) -> p f t", f=8)))
                V(lambda v: v.tensor_copy(s_abf[:], s_x1[:]))
                wait("tensor", ch_v)
                for ot in range(8):
                    T(lambda t, ot=ot: t.matmul(ps_E[0:1, 0:QT], s_ones[:],
                                                s_abf[:, QT * ot:QT * (ot + 1)],
                                                start=(ot == 0), stop=(ot == 7)))
                wait("scalar", ch_v)
                for ot in range(8):
                    S(lambda s, ot=ot: s.activation(s_sq[:, 0:QT],
                                                    s_abf[:, QT * ot:QT * (ot + 1)], AF.Square))
                    wait("tensor", ch_s)
                    T(lambda t, ot=ot: t.matmul(ps_F[0:1, 0:QT], s_ones[:], s_sq[:, 0:QT],
                                                start=(ot == 0), stop=(ot == 7)))
                    wait("scalar", ch_t)
                wait("vector", ch_t, ch_sld)
                V(lambda v: v.tensor_copy(s_row_a[0:1, 0:QT], ps_E[0:1, 0:QT]))
                V(lambda v: v.tensor_copy(s_row_b[0:1, 0:QT], ps_F[0:1, 0:QT]))
                wait("scalar", ch_v)
                row_store(seg, 10, s_row_a, w=QT)
                row_store(seg, 11, s_row_b, w=QT)
                wait("scalar", ch_sld, ch_v)
                row_bcast(seg, 10, s_fq_a, w=QT)
                row_bcast(seg, 11, s_fq_b, w=QT)
                ln_fields(s_fq_a, s_fq_b, s_fq_c, w=QT)
                wait("vector", ch_s, ch_ld)
                for ot in range(8):
                    V(lambda v, ot=ot: v.tensor_sub(s_x1[:, QT * ot:QT * (ot + 1)],
                                                    s_x1[:, QT * ot:QT * (ot + 1)], s_fq_a[:]))
                    V(lambda v, ot=ot: v.tensor_mul(s_abf[:, QT * ot:QT * (ot + 1)],
                                                    s_x1[:, QT * ot:QT * (ot + 1)], s_fq_b[:]))
                wait("sync", ch_v)
                DMA("sync", ch_ld, lambda e, seg=seg: e.dma_start(
                    out=ap3(ln4_dram.tensor, QT * seg, [[MYT, 128], [MYT * 128, 8], [1, QT]]),
                    in_=s_abf[:].rearrange("p (f t) -> p f t", f=8)))
                if seg == DEBUG_SEG:
                    wait("gpsimd", ch_t, ch_v, ch_s, ch_ld, ch_sld)
                    for nm, src, w in (("obf", s_obf, 2 * SEG), ("ocat", s_ocat, 8 * QT),
                                       ("attn", s_attn, 8 * QT), ("x1", s_x1, 8 * QT),
                                       ("G", s_G, TW), ("vtm", s_vtm, NCH * 256),
                                       ("ktm", s_ktm, NCH * 256), ("oag", s_oag, 8 * SEG),
                                       ("qkv", s_qkv, 6 * SEG), ("lnx", s_lnx, 8 * SEG),
                                       ("ema", s_ema, 2 * SEG), ("frq", s_f_rq, SEG),
                                       ("aT", s_aT, TW), ("emin", s_f_tmp, 2),
                                       ("ln2f", s_f_g1m, 2 * QT),
                                       ("uem", s_uema, 2 * SEG), ("fg", s_f_g, SEG)):
                        DMA("gpsimd", ch_gld, lambda e, nm=nm, src=src, w=w: e.dma_start(
                            out=ap3(dbg[nm].tensor, 0, [[8 * SEG, 128], [1, w]]),
                            in_=src[:, 0:w]))
                    wait("gpsimd", ch_gld)

            wait("tensor")
            wait("vector")
            wait("scalar")
            wait("sync")
            wait("gpsimd")
            flush_events()

        with ExitStack() as c4:
            s_ln4 = sbuf("s_ln4", [128, 8 * PBT], c=c4)
            s_z = sbuf("s_z", [128, 16 * PBT], c=c4)
            s_wc = sbuf("s_wc", [128, 16 * 1024], c=c4)
            s_acc = sbuf("s_acc", [128, 8 * MYT], F32, c=c4)
            s_sq2 = sbuf("s_sq2", [128, 512], c=c4)
            s_fb_a = sbuf("s_fb_a", [128, PBT], F32, c=c4)
            s_fb_b = sbuf("s_fb_b", [128, PBT], F32, c=c4)
            s_fb_c = sbuf("s_fb_c", [128, PBT], F32, c=c4)
            s_x2 = sbuf("s_x2", [128, PBT], F32, c=c4)
            s_xt1 = sbuf("s_xt1", [128, PBT], F32, c=c4)

            ps_z0 = c4.enter_context(nc.psum_tensor("ps_z0", [128, 1024], F32))
            ps_z1 = c4.enter_context(nc.psum_tensor("ps_z1", [128, 1024], F32))
            ps_c0 = c4.enter_context(nc.psum_tensor("ps_c0", [128, 512], F32))
            ps_c1 = c4.enter_context(nc.psum_tensor("ps_c1", [128, 512], F32))
            ps_r = c4.enter_context(nc.psum_tensor("ps_r", [128, 512], F32))

            ln4v = s_ln4[:].rearrange("p (f t) -> p f t", f=8)

            for e8 in range(8):
                wait("gpsimd", ch_t, ch_v, ch_s)
                wait("tensor", ch_v, ch_s)
                DMA("gpsimd", ch_gld, lambda e, e8=e8: e.dma_start(
                    out=s_wc[:].rearrange("p (k c) -> p k c", k=16),
                    in_=ap3(wct.tensor, 2048 * 1024 * e8,
                            [[1024, 128], [1024 * 128, 16], [1, 1024]])))
                for bi in range(2):
                    tb0 = PBT * bi
                    wait("sync", ch_t)
                    DMA("sync", ch_ld, lambda e, tb0=tb0: e.dma_start(
                        out=ln4v,
                        in_=ap3(ln4_dram.tensor, tb0, [[MYT, 128], [MYT * 128, 8], [1, PBT]])))
                    wait("tensor", ch_ld, ch_v)
                    for zi in range(16):
                        zt = 16 * e8 + zi
                        bp = 32 * ((zt % 32) % 4)
                        ft = (zt % 32) // 4
                        tgt = ps_z0 if (zi % 2 == 0) else ps_z1
                        for th in range(2):
                            T(lambda t, zt=zt, bp=bp, ft=ft, th=th, tgt=tgt, tb0=tb0: t.matmul(
                                tgt[:, 512 * th:512 * (th + 1)],
                                s_compT[bp:bp + 32,
                                        128 * ((zt % 32) // 4) + 1024 * (zt // 32):
                                        128 * ((zt % 32) // 4) + 1024 * (zt // 32) + 128],
                                ln4v[bp:bp + 32, ft, 512 * th:512 * (th + 1)],
                                start=True, stop=True, tile_position=(bp, 0)))
                        wait("scalar", ch_t)
                        S(lambda s, zi=zi, tgt=tgt: s.activation(
                            s_z[:, PBT * zi:PBT * (zi + 1)], tgt[:, 0:PBT], AF.Gelu))
                        if zi >= 1:
                            wait("tensor", ch_s)
                    wait("tensor", ch_s)
                    wait("tensor", ch_gld)
                    for ot in range(8):
                        for th in range(2):
                            tgt = ps_c0 if ((ot * 2 + th) % 2 == 0) else ps_c1
                            for kt in range(16):
                                T(lambda t, ot=ot, th=th, kt=kt, tgt=tgt: t.matmul(
                                    tgt[:, 0:512],
                                    s_wc[:, 1024 * kt + 128 * ot:1024 * kt + 128 * (ot + 1)],
                                    s_z[:, PBT * kt + 512 * th:PBT * kt + 512 * (th + 1)],
                                    start=(kt == 0), stop=(kt == 15)))
                            wait("vector", ch_t)
                            col = MYT * ot + tb0 + 512 * th
                            if e8 == 0:
                                V(lambda v, col=col, tgt=tgt: v.tensor_copy(
                                    s_acc[:, col:col + 512], tgt[:, 0:512]))
                            else:
                                V(lambda v, col=col, tgt=tgt: v.tensor_add(
                                    s_acc[:, col:col + 512], s_acc[:, col:col + 512],
                                    tgt[:, 0:512]))
                            if (ot * 2 + th) % 2 == 1:
                                wait("tensor", ch_v)
                    wait("tensor", ch_v)


            for bi in range(2):
                tb0 = PBT * bi
                for th in range(2):
                    for ot in range(8):
                        col = MYT * ot + tb0 + 512 * th
                        wait("vector", ch_t, ch_s)
                        V(lambda v, col=col: v.tensor_copy(s_sq2[:], s_acc[:, col:col + 512]))
                        wait("scalar", ch_v)
                        S(lambda s: s.activation(s_sq[:, 0:512], s_sq2[:], AF.Square))
                        wait("tensor", ch_v, ch_s)
                        T(lambda t, ot=ot: t.matmul(ps_r[0:1, 0:512], s_ones[:], s_sq2[:],
                                                    start=(ot == 0), stop=(ot == 7)))
                        T(lambda t, ot=ot: t.matmul(ps_c0[0:1, 0:512], s_ones[:], s_sq[:, 0:512],
                                                    start=(ot == 0), stop=(ot == 7)))
                    wait("vector", ch_t, ch_sld)
                    V(lambda v: v.tensor_copy(s_row_a[0:1, 0:512], ps_r[0:1, 0:512]))
                    V(lambda v: v.tensor_copy(s_row_b[0:1, 0:512], ps_c0[0:1, 0:512]))
                    wait("scalar", ch_v)
                    DMA("scalar", ch_sld, lambda e, bi=bi, th=th: e.dma_start(
                        out=ap3(rowbuf2.tensor, (bi * 4 + 0) * PBT + 512 * th, [[1, 512]]),
                        in_=s_row_a[0:1, 0:512]))
                    DMA("scalar", ch_sld, lambda e, bi=bi, th=th: e.dma_start(
                        out=ap3(rowbuf2.tensor, (bi * 4 + 1) * PBT + 512 * th, [[1, 512]]),
                        in_=s_row_b[0:1, 0:512]))
                wait("scalar", ch_sld, ch_v)
                DMA("scalar", ch_sld, lambda e, bi=bi: e.dma_start(
                    out=s_fb_a[:],
                    in_=ap3(rowbuf2.tensor, (bi * 4 + 0) * PBT, [[0, 128], [1, PBT]])))
                DMA("scalar", ch_sld, lambda e, bi=bi: e.dma_start(
                    out=s_fb_b[:],
                    in_=ap3(rowbuf2.tensor, (bi * 4 + 1) * PBT, [[0, 128], [1, PBT]])))
                wait("vector", ch_sld)
                V(lambda v: v.tensor_scalar(s_fb_a[:], s_fb_a[:], cst(C_I1024), None, ALU.mult))
                V(lambda v: v.tensor_scalar(s_fb_b[:], s_fb_b[:], cst(C_I1024), None, ALU.mult))
                wait("scalar", ch_v)
                S(lambda s: s.activation(s_fb_c[:], s_fb_a[:], AF.Square))
                wait("vector", ch_s)
                V(lambda v: v.tensor_sub(s_fb_b[:], s_fb_b[:], s_fb_c[:]))
                wait("scalar", ch_v)
                S(lambda s: s.activation(s_fb_b[:], s_fb_b[:], AF.Ln, bias=cst(C_EPS_LN)))
                S(lambda s: s.activation(s_fb_b[:], s_fb_b[:], AF.Exp, scale=cst(C_NHALF)))
                for ot in range(8):
                    col = MYT * ot + tb0
                    wait("sync", ch_v)
                    DMA("sync", ch_ld, lambda e, ot=ot, tb0=tb0: e.dma_start(
                        out=s_xt1[:],
                        in_=ap3(x1_dram.tensor, MYT * 128 * ot + tb0, [[MYT, 128], [1, PBT]])))
                    wait("vector", ch_s, ch_ld)
                    V(lambda v, col=col: v.tensor_sub(s_x2[:], s_acc[:, col:col + PBT], s_fb_a[:]))
                    V(lambda v: v.tensor_mul(s_x2[:], s_x2[:], s_fb_b[:]))
                    V(lambda v: v.tensor_add(s_x2[:], s_x2[:], s_xt1[:]))
                    wait("sync", ch_v)
                    DMA("sync", ch_ld, lambda e, ot=ot, tb0=tb0: e.dma_start(
                        out=ap3(out.tensor, MYT * 128 * ot + tb0, [[MYT, 128], [1, PBT]]),
                        in_=s_x2[:]))
            wait("sync")
            wait("gpsimd")
            flush_events()

    return nc


_NC = None
_LAST = None


def _get_nc():
    global _NC
    if _NC is None:
        _NC = build_program()
    return _NC


def kernel(**inputs):
    f32 = np.float32
    hs = np.asarray(inputs["L0_attn_hidden_states"], f32)
    Wq = np.asarray(inputs["L0_attn_Wq"], f32)
    Wk = np.asarray(inputs["L0_attn_Wk"], f32)
    Wv = np.asarray(inputs["L0_attn_Wv"], f32)
    Wb = np.asarray(inputs["L0_attn_Wb"], f32)
    Wdec = np.asarray(inputs["L0_attn_Wdec"], f32)
    Wmix = np.asarray(inputs["L0_attn_Wmix"], f32)
    mix_bias = np.asarray(inputs["L0_attn_mix_bias"], f32)
    wq_conv = np.asarray(inputs["L0_attn_wq_conv"], f32)
    wk_conv = np.asarray(inputs["L0_attn_wk_conv"], f32)
    wv_conv = np.asarray(inputs["L0_attn_wv_conv"], f32)
    onorm_w = np.asarray(inputs["L0_attn_onorm_w"], f32)
    Wo = np.asarray(inputs["L0_attn_Wo"], f32)
    maker = np.asarray(inputs["L0_ffn_rail_maker"], f32)
    up = np.asarray(inputs["L0_ffn_rail_up_proj"], f32)
    Wc = np.asarray(inputs["L0_ffn_w_cproj"], f32)

    consts = np.tile(np.array([1e-5, 1e-6, 1e-5, -0.5, 1.0 / 1024.0, 1.0 / 256.0, 1.0, 0.0],
                              f32), (128, 1))
    wct = np.ascontiguousarray(Wc.T)
    wot = np.ascontiguousarray(Wo.T)

    mk_bd = np.zeros((32, 128, 128), f32)
    up_bd = np.zeros((32, 128, 512), f32)
    for g in range(32):
        hh, rg = g // 8, g % 8
        for rr in range(8):
            r = 8 * rg + rr
            mk_bd[g, 16 * rr:16 * rr + 16, 16 * rr:16 * rr + 16] = maker[hh, r].T
            up_bd[g, 16 * rr:16 * rr + 16, 64 * rr:64 * rr + 64] = up[hh, r].T

    x_pad = [None, None]
    for b in range(2):
        xt = np.ascontiguousarray(hs[b].T)
        x_pad[b] = np.ascontiguousarray(
            np.concatenate([np.zeros((D, 3), f32), xt], axis=1))

    in_maps = []
    for c in range(8):
        b, h = c // 4, c % 4
        xt = x_pad[b][:, 3:]
        x_mine = np.ascontiguousarray(np.concatenate(
            [xt[:, SEG * s + QT * h:SEG * s + QT * h + QT] for s in range(NSEG)], axis=1))
        sl = slice(DK * h, DK * (h + 1))
        wp = np.ascontiguousarray(np.concatenate(
            [Wq.T[:, sl], Wk.T[:, sl], Wv.T[:, sl],
             Wb.T[:, h:h + 1], Wdec.T[:, h:h + 1], Wmix.T[:, h:h + 1]], axis=1))
        cw = np.ascontiguousarray(np.concatenate(
            [wq_conv[sl], wk_conv[sl], wv_conv[sl]], axis=0))
        ms = np.zeros((128, 4), f32)
        ms[:, h] = 1.0
        in_maps.append({
            "x_in": x_pad[b],
            "x_mine": x_mine,
            "wproj": wp,
            "convw": cw,
            "wot": wot,
            "wct": wct,
            "mk_bd": mk_bd,
            "up_bd": up_bd,
            "onormw": np.ascontiguousarray(onorm_w.reshape(2, 128).T),
            "mixb": mix_bias[h].reshape(1, 1).astype(f32),
            "consts": consts,
            "msel": ms,
        })

    nc = _get_nc()
    res = run_bass_kernel_spmd(nc, in_maps, core_ids=list(range(8)))
    global _LAST
    _LAST = res

    out_full = np.empty((2, L, D), f32)
    for c in range(8):
        b, h = c // 4, c % 4
        sh = res.results[c]["out"]
        sh = np.asarray(sh, f32).T.reshape(NSEG, QT, D)
        out_full[b].reshape(NSEG, 4, QT, D)[:, h] = sh
    return out_full


# version 03
# speedup vs baseline: 1.1048x; 1.1048x over previous
import numpy as np
import ml_dtypes
import concourse.bass as bass
import concourse.mybir as mybir
from concourse.bass_utils import run_bass_kernel_spmd
from contextlib import ExitStack

F32 = mybir.dt.float32
BF16 = mybir.dt.bfloat16
AF = mybir.ActivationFunctionType
ALU = mybir.AluOpType

B, L, D, H = 2, 8192, 1024, 4
DK = 256
CH = 128
SEG = 512
NSEG = L // SEG
NCH = SEG // CH
QT = SEG // 4
MYT = L // 4
PBT = 1024
TW = NCH * CH

C_EPS_LN, C_EPS_L2, C_EPS_RMS, C_NHALF, C_I1024, C_I256, C_ONE, C_ZERO = range(8)

DEBUG_SEG = -1


def build_program():
    nc = bass.Bass("TRN2", target_bir_lowering=False)

    x_in = nc.dram_tensor("x_in", [D, L + 3], F32, kind="ExternalInput").ap()
    x_mine = nc.dram_tensor("x_mine", [D, MYT], F32, kind="ExternalInput").ap()
    wproj = nc.dram_tensor("wproj", [D, 771], F32, kind="ExternalInput").ap()
    convw = nc.dram_tensor("convw", [3 * DK, 4], F32, kind="ExternalInput").ap()
    wot = nc.dram_tensor("wot", [D, D], F32, kind="ExternalInput").ap()
    wct = nc.dram_tensor("wct", [16 * D, D], F32, kind="ExternalInput").ap()
    mk_bd = nc.dram_tensor("mk_bd", [32, 128, 128], F32, kind="ExternalInput").ap()
    up_bd = nc.dram_tensor("up_bd", [32, 128, 512], F32, kind="ExternalInput").ap()
    onormw = nc.dram_tensor("onormw", [128, 2], F32, kind="ExternalInput").ap()
    mixb = nc.dram_tensor("mixb", [1, 1], F32, kind="ExternalInput").ap()
    consts = nc.dram_tensor("consts", [128, 8], F32, kind="ExternalInput").ap()
    msel = nc.dram_tensor("msel", [128, 4], F32, kind="ExternalInput").ap()
    out = nc.dram_tensor("out", [D, MYT], F32, kind="ExternalOutput").ap()

    rowbuf = nc.dram_tensor("rowbuf", [NSEG, 16, SEG], F32).ap()
    rowbuf2 = nc.dram_tensor("rowbuf2", [2, 4, PBT], F32).ap()
    x1_dram = nc.dram_tensor("x1_dram", [D, MYT], F32).ap()
    ln4_dram = nc.dram_tensor("ln4_dram", [D, MYT], BF16).ap()
    cc_in = nc.dram_tensor("cc_in", [2, 128, SEG], BF16).ap()
    cc_out = nc.dram_tensor("cc_out", [4, 2, 128, SEG], BF16).ap()

    dbg = {}
    if DEBUG_SEG >= 0:
        for nm in ("obf", "ocat", "attn", "x1", "G", "vtm", "ktm", "oag", "qkv", "lnx",
                   "ema", "frq", "aT", "emin", "uem", "fg", "ln2f"):
            dbg[nm] = nc.dram_tensor("dbg_" + nm, [128, 8 * SEG], F32, kind="ExternalOutput").ap()

    events = {"tensor": [], "vector": [], "scalar": [], "gpsimd": [], "sync": []}
    ap3 = lambda h, off, dims: bass.AP(h, off, dims)

    with ExitStack() as ctx:
        def sbuf(name, shape, dt=BF16, c=None):
            return (c or ctx).enter_context(nc.sbuf_tensor(name, shape, dt))

        class Chan:
            def __init__(self, name):
                self.sem = ctx.enter_context(nc.semaphore(name))
                self.n = 0

        ch_ld = Chan("ch_ld")
        ch_sld = Chan("ch_sld")
        ch_gld = Chan("ch_gld")
        ch_t = Chan("ch_t")
        ch_v = Chan("ch_v")
        ch_s = Chan("ch_s")
        ch_cc = Chan("ch_cc")
        ch_g = Chan("ch_g")
        CHANNELS = [ch_ld, ch_sld, ch_gld, ch_t, ch_v, ch_s, ch_cc, ch_g]

        def wait_at(eng_name, ch, n):
            if n > 0:
                def fn(eng, sem=ch.sem, n=n):
                    eng.wait_ge(sem, n)
                events[eng_name].append(fn)

        def wait(eng_name, *chs):
            if not chs:
                chs = CHANNELS
            snap = [(c.sem, c.n) for c in chs if c.n > 0]

            def fn(eng, snap=snap):
                for sem, n in snap:
                    eng.wait_ge(sem, n)
            events[eng_name].append(fn)

        def _mk(eng_name, ch):
            def emit(fn):
                def g(e, fn=fn):
                    fn(e).then_inc(ch.sem, 1)
                ch.n += 1
                events[eng_name].append(g)
            return emit

        T = _mk("tensor", ch_t)
        V = _mk("vector", ch_v)
        S = _mk("scalar", ch_s)
        G = _mk("gpsimd", ch_g)
        CC = _mk("gpsimd", ch_cc)

        def DMA(eng_name, ch, fn):
            def g(e, fn=fn, ch=ch):
                fn(e).then_inc(ch.sem, 16)
            ch.n += 16
            events[eng_name].append(g)

        def flush_events():
            with nc.Block() as block:
                for name in ("sync", "gpsimd", "scalar", "vector", "tensor"):
                    evs = tuple(events[name])
                    events[name] = []
                    if evs:
                        getattr(block, name)(lambda eng, evs=evs: [fn(eng) for fn in evs])

        s_wproj = sbuf("s_wproj", [128, 8 * 771])
        s_convw = sbuf("s_convw", [128, 24], F32)
        s_compT = sbuf("s_compT", [128, 32 * 128])
        s_onorm = sbuf("s_onorm", [128, 2], F32)
        s_consts = sbuf("s_consts", [128, 8], F32)
        s_msel = sbuf("s_msel", [128, 4], F32)
        s_mixbias = sbuf("s_mixbias", [3, 1], F32)
        s_ones = sbuf("s_ones", [128, 1])
        s_ident = sbuf("s_ident", [128, 128])
        s_identf = sbuf("s_identf", [128, 128], F32)
        s_mLneg = sbuf("s_mLneg", [128, 128])
        s_mUneg = sbuf("s_mUneg", [128, 128])
        s_mUinc = sbuf("s_mUinc", [128, 128])
        s_S = sbuf("s_S", [128, 512], F32)
        s_Sbf = sbuf("s_Sbf", [128, 512])
        s_ema_st = sbuf("s_ema_st", [128, 2], F32)
        s_one1 = sbuf("s_one1", [1, 1], F32)
        s_row_a = sbuf("s_row_a", [1, SEG], F32)
        s_row_b = sbuf("s_row_b", [1, SEG], F32)
        s_row3 = sbuf("s_row3", [3, SEG], F32)
        s_sq = sbuf("s_sq", [128, SEG])
        s_sqB = sbuf("s_sqB", [128, SEG])

        cst = lambda j: s_consts[:, j:j + 1]

        def prelude(g):
            g.memset(s_ident[:], 1.0)
            g.affine_select(s_ident[:], s_ident[:], [[-1, 128]], ALU.is_equal, 0.0,
                            base=0, channel_multiplier=1)
            g.memset(s_identf[:], 1.0)
            g.affine_select(s_identf[:], s_identf[:], [[-1, 128]], ALU.is_equal, 0.0,
                            base=0, channel_multiplier=1)
            g.memset(s_mLneg[:], -1.0)
            g.affine_select(s_mLneg[:], s_mLneg[:], [[-1, 128]], ALU.is_gt, 0.0,
                            base=0, channel_multiplier=1)
            g.memset(s_mUneg[:], -1.0)
            g.affine_select(s_mUneg[:], s_mUneg[:], [[1, 128]], ALU.is_gt, 0.0,
                            base=0, channel_multiplier=-1)
            g.memset(s_mUinc[:], 1.0)
            g.affine_select(s_mUinc[:], s_mUinc[:], [[1, 128]], ALU.is_ge, 0.0,
                            base=0, channel_multiplier=-1)
            g.memset(s_ones[:], 1.0)
            g.memset(s_one1[:], 1.0)
            g.memset(s_S[:], 0.0)
            g.memset(s_Sbf[:], 0.0)
            g.memset(s_ema_st[:], 0.0)
            return g.memset(s_mixbias[:], 0.0)
        G(prelude)

        DMA("sync", ch_ld, lambda e: e.dma_start(out=s_consts[:], in_=consts))
        DMA("sync", ch_ld, lambda e: e.dma_start(out=s_msel[:], in_=msel))
        DMA("sync", ch_ld, lambda e: e.dma_start(
            out=s_convw[:].rearrange("p (t c) -> p t c", t=6),
            in_=ap3(convw.tensor, 0, [[4, 128], [4 * 128, 6], [1, 4]])))
        DMA("sync", ch_ld, lambda e: e.dma_start(out=s_onorm[:], in_=onormw))
        wait("sync", ch_g)
        DMA("sync", ch_ld, lambda e: e.dma_start(out=s_mixbias[2:3, 0:1], in_=mixb))
        DMA("gpsimd", ch_gld, lambda e: e.dma_start(
            out=s_wproj[:].rearrange("p (t c) -> p t c", t=8),
            in_=ap3(wproj.tensor, 0, [[771, 128], [771 * 128, 8], [1, 771]])))

        with ExitStack() as c2:
            s_mk = sbuf("s_mk", [128, 32 * 128], c=c2)
            s_up = sbuf("s_up", [128, 32 * 512], c=c2)
            ps_A0 = c2.enter_context(nc.psum_tensor("ps_A0", [128, 1024], F32))
            ps_B0 = c2.enter_context(nc.psum_tensor("ps_B0", [128, 1024], F32))
            DMA("gpsimd", ch_gld, lambda e: e.dma_start(
                out=s_mk[:].rearrange("p (g c) -> p g c", g=32),
                in_=ap3(mk_bd.tensor, 0, [[128, 128], [128 * 128, 32], [1, 128]])))
            DMA("gpsimd", ch_gld, lambda e: e.dma_start(
                out=s_up[:].rearrange("p (g c) -> p g c", g=32),
                in_=ap3(up_bd.tensor, 0, [[512, 128], [512 * 128, 32], [1, 512]])))
            wait("tensor", ch_gld, ch_g)
            for g_i in range(32):
                tgt = ps_A0 if (g_i % 2 == 0) else ps_B0
                T(lambda t, g_i=g_i, tgt=tgt: t.matmul(
                    tgt[:, 0:512], s_mk[:, 128 * g_i:128 * (g_i + 1)],
                    s_up[:, 512 * g_i:512 * (g_i + 1)], start=True, stop=True))
                wait("vector", ch_t)
                for j in range(4):
                    V(lambda v, g_i=g_i, j=j, tgt=tgt: v.tensor_copy(
                        s_compT[32 * j:32 * (j + 1), 128 * g_i:128 * (g_i + 1)],
                        tgt[32 * j:32 * (j + 1), 128 * j:128 * (j + 1)]))
                if g_i % 2 == 1:
                    wait("tensor", ch_v)
            wait("tensor", ch_v)
            wait("sync", ch_v)
            wait("scalar", ch_v)
            wait("gpsimd", ch_v)
            flush_events()

        with ExitStack() as c1:
            s_wot = sbuf("s_wot", [128, 8 * 1024], c=c1)
            DMA("gpsimd", ch_gld, lambda e: e.dma_start(
                out=s_wot[:].rearrange("p (t c) -> p t c", t=8),
                in_=ap3(wot.tensor, 0, [[1024, 128], [1024 * 128, 8], [1, 1024]])))

            s_xs = sbuf("s_xs", [128, 8 * SEG], c=c1)
            s_lnx = sbuf("s_lnx", [128, 8 * SEG], c=c1)
            s_raw = sbuf("s_raw", [128, 6 * (SEG + 3)], c=c1)
            s_qkv = sbuf("s_qkv", [128, 6 * SEG], c=c1)
            s_khat = sbuf("s_khat", [128, 2 * SEG], c=c1)
            s_kb = sbuf("s_kb", [128, 2 * SEG], c=c1)
            s_vb = sbuf("s_vb", [128, 2 * SEG], c=c1)
            s_ktm = sbuf("s_ktm", [128, NCH * 256], c=c1)
            s_vtm = sbuf("s_vtm", [128, NCH * 256], c=c1)
            s_f_m1 = sbuf("s_f_m1", [128, SEG], F32, c=c1)
            s_f_r1 = sbuf("s_f_r1", [128, SEG], F32, c=c1)
            s_f_b = sbuf("s_f_b", [128, SEG], F32, c=c1)
            s_f_g = sbuf("s_f_g", [128, SEG], F32, c=c1)
            s_f_g1m = sbuf("s_f_g1m", [128, SEG], F32, c=c1)
            s_f_mix = sbuf("s_f_mix", [128, SEG], F32, c=c1)
            s_f_rq = sbuf("s_f_rq", [128, SEG], F32, c=c1)
            s_f_rk = sbuf("s_f_rk", [128, SEG], F32, c=c1)
            s_f_rms = sbuf("s_f_rms", [128, SEG], F32, c=c1)
            s_f_tmp = sbuf("s_f_tmp", [128, SEG], F32, c=c1)
            s_A = sbuf("s_A", [128, TW], c=c1)
            s_AT = sbuf("s_AT", [128, TW], c=c1)
            s_X = [sbuf(f"s_X{k}", [128, TW], c=c1) for k in range(6)]
            s_XT = [sbuf(f"s_XT{k}", [128, TW], c=c1) for k in range(2)]
            s_G = sbuf("s_G", [128, TW], c=c1)
            s_aT = sbuf("s_aT", [128, TW], c=c1)
            s_r = sbuf("s_r", [128, 256], c=c1)
            s_u = sbuf("s_u", [128, 256], c=c1)
            s_uema = sbuf("s_uema", [128, 2 * SEG], F32, c=c1)
            s_lcA = sbuf("s_lcA", [1, SEG], F32, c=c1)
            s_lcB = sbuf("s_lcB", [1, SEG], F32, c=c1)
            s_neglc = sbuf("s_neglc", [128, 4], F32, c=c1)
            s_f_lc = sbuf("s_f_lc", [128, SEG], F32, c=c1)
            s_f_p = sbuf("s_f_p", [128, SEG], F32, c=c1)
            s_uetm = sbuf("s_uetm", [128, NCH * 256], F32, c=c1)
            s_mt = sbuf("s_mt", [128, 128], F32, c=c1)
            s_mtf = sbuf("s_mtf", [128, NCH * 128], F32, c=c1)
            s_emtmp = sbuf("s_emtmp", [128, 128], F32, c=c1)
            s_ema = sbuf("s_ema", [128, 2 * SEG], F32, c=c1)
            s_oseg = sbuf("s_oseg", [128, 2 * SEG], F32, c=c1)
            s_obf = sbuf("s_obf", [128, 2 * SEG], c=c1)
            s_oag = sbuf("s_oag", [128, 8 * SEG], c=c1)
            s_ocat = sbuf("s_ocat", [128, 8 * QT], c=c1)
            s_attn = sbuf("s_attn", [128, 8 * QT], F32, c=c1)
            s_abf = sbuf("s_abf", [128, 8 * QT], c=c1)
            s_x1 = sbuf("s_x1", [128, 8 * QT], F32, c=c1)
            s_xres = sbuf("s_xres", [128, 8 * QT], F32, c=c1)
            s_fq_a = sbuf("s_fq_a", [128, QT], F32, c=c1)
            s_fq_b = sbuf("s_fq_b", [128, QT], F32, c=c1)
            s_fq_c = sbuf("s_fq_c", [128, QT], F32, c=c1)

            ps_A = c1.enter_context(nc.psum_tensor("ps_A", [128, 1024], F32))
            ps_B = c1.enter_context(nc.psum_tensor("ps_B", [128, 1024], F32))
            ps_C = c1.enter_context(nc.psum_tensor("ps_C", [128, 512], F32))
            ps_D = c1.enter_context(nc.psum_tensor("ps_D", [128, 512], F32))
            ps_E = c1.enter_context(nc.psum_tensor("ps_E", [128, 512], F32))
            ps_F = c1.enter_context(nc.psum_tensor("ps_F", [128, 512], F32))

            def row_store(seg, r0, src, nr=1, w=SEG):
                DMA("scalar", ch_sld, lambda e, seg=seg, r0=r0, src=src, nr=nr, w=w: e.dma_start(
                    out=ap3(rowbuf.tensor, (seg * 16 + r0) * SEG, [[SEG, nr], [1, w]]),
                    in_=src[0:nr, 0:w]))

            def row_bcast(seg, row, dst, w=SEG):
                DMA("scalar", ch_sld, lambda e, seg=seg, row=row, dst=dst, w=w: e.dma_start(
                    out=dst[:, 0:w],
                    in_=ap3(rowbuf.tensor, (seg * 16 + row) * SEG, [[0, 128], [1, w]])))

            def ln_fields(mf, vf, tmp, w=SEG):
                wait("vector", ch_sld)
                V(lambda v: v.tensor_scalar(mf[:, 0:w], mf[:, 0:w], cst(C_I1024), None, ALU.mult))
                V(lambda v: v.tensor_scalar(vf[:, 0:w], vf[:, 0:w], cst(C_I1024), None, ALU.mult))
                wait("scalar", ch_v)
                S(lambda s: s.activation(tmp[:, 0:w], mf[:, 0:w], AF.Square))
                wait("vector", ch_s)
                V(lambda v: v.tensor_sub(vf[:, 0:w], vf[:, 0:w], tmp[:, 0:w]))
                wait("scalar", ch_v)
                S(lambda s: s.activation(vf[:, 0:w], vf[:, 0:w], AF.Ln, bias=cst(C_EPS_LN)))
                S(lambda s: s.activation(vf[:, 0:w], vf[:, 0:w], AF.Exp, scale=cst(C_NHALF)))

            for seg in range(NSEG):
                c0 = seg * SEG

                wait("gpsimd", ch_t, ch_s, ch_v)
                wait("tensor", ch_v, ch_s, ch_sld, ch_ld, ch_gld)
                wait("vector", ch_t, ch_s, ch_sld, ch_ld, ch_gld)
                wait("scalar", ch_t, ch_v, ch_sld, ch_ld, ch_gld)
                DMA("gpsimd", ch_gld, lambda e, c0=c0: e.dma_start(
                    out=s_xs[:].rearrange("p (t c) -> p t c", t=8),
                    in_=ap3(x_in.tensor, 3 + c0, [[L + 3, 128], [(L + 3) * 128, 8], [1, SEG]])))

                wait("tensor", ch_gld, ch_ld, ch_g)
                wait("scalar", ch_gld)
                mm_done = [0, 0]
                for dt_i in range(8):
                    buf = s_sq if dt_i % 2 == 0 else s_sqB
                    wait_at("scalar", ch_t, mm_done[dt_i % 2])
                    S(lambda s, dt_i=dt_i, buf=buf: s.activation(
                        buf[:], s_xs[:, SEG * dt_i:SEG * (dt_i + 1)], AF.Square))
                    wait("tensor", ch_s)
                    T(lambda t, dt_i=dt_i, buf=buf: t.matmul(
                        ps_F[0:1, 0:SEG], s_ones[:], buf[:],
                        start=(dt_i == 0), stop=(dt_i == 7)))
                    mm_done[dt_i % 2] = ch_t.n
                for dt_i in range(8):
                    T(lambda t, dt_i=dt_i: t.matmul(ps_E[0:1, 0:SEG], s_ones[:],
                                                    s_xs[:, SEG * dt_i:SEG * (dt_i + 1)],
                                                    start=(dt_i == 0), stop=(dt_i == 7)))
                wait("vector", ch_t, ch_sld)
                V(lambda v: v.tensor_copy(s_row_a[0:1, :], ps_E[0:1, 0:SEG]))
                V(lambda v: v.tensor_copy(s_row_b[0:1, :], ps_F[0:1, 0:SEG]))
                wait("scalar", ch_v)
                row_store(seg, 0, s_row_a)
                row_store(seg, 1, s_row_b)
                wait("scalar", ch_sld)
                row_bcast(seg, 0, s_f_m1)
                row_bcast(seg, 1, s_f_r1)
                ln_fields(s_f_m1, s_f_r1, s_f_tmp)
                wait("vector", ch_s)
                for dt_i in range(8):
                    V(lambda v, dt_i=dt_i: v.tensor_sub(
                        s_lnx[:, SEG * dt_i:SEG * (dt_i + 1)],
                        s_xs[:, SEG * dt_i:SEG * (dt_i + 1)], s_f_m1[:]))
                    V(lambda v, dt_i=dt_i: v.tensor_mul(
                        s_lnx[:, SEG * dt_i:SEG * (dt_i + 1)],
                        s_lnx[:, SEG * dt_i:SEG * (dt_i + 1)], s_f_r1[:]))

                if seg > 0:
                    V(lambda v: v.tensor_copy(
                        s_raw[:].rearrange("p (t c) -> p t c", t=6)[:, :, 0:3],
                        s_raw[:].rearrange("p (t c) -> p t c", t=6)[:, :, SEG:SEG + 3]))

                wait("tensor", ch_v)
                pevac = [0, 0]
                for ot in range(6):
                    tgt = ps_A if (ot % 2 == 0) else ps_B
                    wait_at("tensor", ch_v, pevac[ot % 2])
                    for kt in range(8):
                        T(lambda t, ot=ot, kt=kt, tgt=tgt: t.matmul(
                            tgt[:, 0:SEG],
                            s_wproj[:, 771 * kt + 128 * ot:771 * kt + 128 * (ot + 1)],
                            s_lnx[:, SEG * kt:SEG * (kt + 1)],
                            start=(kt == 0), stop=(kt == 7)))
                    wait("vector", ch_t)
                    V(lambda v, ot=ot, tgt=tgt: v.tensor_copy(
                        s_raw[:, (SEG + 3) * ot + 3:(SEG + 3) * (ot + 1)], tgt[:, 0:SEG]))
                    pevac[ot % 2] = ch_v.n
                for kt in range(8):
                    T(lambda t, kt=kt: t.matmul(
                        ps_E[0:3, 0:SEG], s_wproj[:, 771 * kt + 768:771 * kt + 771],
                        s_lnx[:, SEG * kt:SEG * (kt + 1)], start=(kt == 0), stop=(kt == 7)))
                wait("scalar", ch_t, ch_sld)
                S(lambda s: s.activation(s_row3[0:3, :], ps_E[0:3, 0:SEG], AF.Sigmoid,
                                         bias=s_mixbias[:]))
                wait("scalar", ch_s)
                row_store(seg, 2, s_row3, nr=3)
                wait("scalar", ch_sld)
                row_bcast(seg, 2, s_f_b)
                row_bcast(seg, 3, s_f_g)
                row_bcast(seg, 4, s_f_mix)
                DMA("scalar", ch_sld, lambda e, seg=seg: e.dma_start(
                    out=s_lcA[0:1, :],
                    in_=ap3(rowbuf.tensor, (seg * 16 + 3) * SEG, [[SEG, 1], [1, SEG]])))
                wait("scalar", ch_sld)
                S(lambda s: s.activation(s_lcA[0:1, :], s_lcA[0:1, :], AF.Ln))
                wait("vector", ch_s)
                V(lambda v: v.tensor_scalar_max(s_lcA[0:1, :], s_lcA[0:1, :], -80.0))
                cur, oth = s_lcA, s_lcB
                sh = 1
                while sh < CH:
                    V(lambda v, cur=cur, oth=oth, sh=sh: v.tensor_copy(
                        ap3(oth, 0, [[SEG, 1], [CH, NCH], [1, sh]]),
                        ap3(cur, 0, [[SEG, 1], [CH, NCH], [1, sh]])))
                    V(lambda v, cur=cur, oth=oth, sh=sh: v.tensor_add(
                        ap3(oth, sh, [[SEG, 1], [CH, NCH], [1, CH - sh]]),
                        ap3(cur, sh, [[SEG, 1], [CH, NCH], [1, CH - sh]]),
                        ap3(cur, 0, [[SEG, 1], [CH, NCH], [1, CH - sh]])))
                    cur, oth = oth, cur
                    sh *= 2
                s_lc = cur
                wait("tensor", ch_v)
                for c in range(NCH):
                    T(lambda t, c=c, s_lc=s_lc: t.matmul(
                        ps_F[:, c:c + 1], s_lc[0:1, 128 * c:128 * (c + 1)], s_one1[0:1, 0:1],
                        start=True, stop=True))
                wait("vector", ch_t)
                V(lambda v: v.tensor_scalar(s_neglc[:], ps_F[:, 0:4], -1.0, None, ALU.mult))
                wait("tensor", ch_v)
                wait("scalar", ch_v)
                row_store(seg, 12, s_lc)
                wait("scalar", ch_sld)
                row_bcast(seg, 12, s_f_lc)

                wait("vector", ch_t)
                for ot in range(6):
                    rb = (SEG + 3) * ot
                    V(lambda v, ot=ot, rb=rb: v.tensor_scalar(
                        s_qkv[:, SEG * ot:SEG * (ot + 1)], s_raw[:, rb:rb + SEG],
                        s_convw[:, 4 * ot:4 * ot + 1], None, ALU.mult))
                    for j in (1, 2, 3):
                        V(lambda v, ot=ot, rb=rb, j=j: v.scalar_tensor_tensor(
                            s_qkv[:, SEG * ot:SEG * (ot + 1)],
                            s_raw[:, rb + j:rb + j + SEG],
                            s_convw[:, 4 * ot + j:4 * ot + j + 1],
                            s_qkv[:, SEG * ot:SEG * (ot + 1)], ALU.mult, ALU.add))
                    wait("scalar", ch_v)
                    S(lambda s, ot=ot: s.activation(
                        s_qkv[:, SEG * ot:SEG * (ot + 1)],
                        s_qkv[:, SEG * ot:SEG * (ot + 1)], AF.Silu))

                wait("scalar", ch_s)
                for base, row in ((0, 5), (2, 6)):
                    wait("scalar", ch_t)
                    for i in range(2):
                        buf = s_sq if i % 2 == 0 else s_sqB
                        S(lambda s, base=base, i=i, buf=buf: s.activation(
                            buf[:], s_qkv[:, SEG * (base + i):SEG * (base + i + 1)], AF.Square))
                        wait("tensor", ch_s)
                        T(lambda t, i=i, buf=buf: t.matmul(ps_F[0:1, 0:SEG], s_ones[:], buf[:],
                                                           start=(i == 0), stop=(i == 1)))
                    dst = s_row_a if row == 5 else s_row_b
                    wait("vector", ch_t, ch_sld)
                    V(lambda v, dst=dst: v.tensor_copy(dst[0:1, :], ps_F[0:1, 0:SEG]))
                    wait("tensor", ch_v)
                wait("scalar", ch_v)
                row_store(seg, 5, s_row_a)
                row_store(seg, 6, s_row_b)
                wait("scalar", ch_sld)
                row_bcast(seg, 5, s_f_rq)
                row_bcast(seg, 6, s_f_rk)
                wait("scalar", ch_sld)
                for f in (s_f_rq, s_f_rk):
                    S(lambda s, f=f: s.activation(f[:], f[:], AF.Ln, bias=cst(C_EPS_L2)))
                    S(lambda s, f=f: s.activation(f[:], f[:], AF.Exp, scale=cst(C_NHALF)))

                wait("vector", ch_s, ch_ld, ch_sld)
                for i in range(2):
                    V(lambda v, i=i: v.tensor_mul(
                        s_khat[:, SEG * i:SEG * (i + 1)],
                        s_qkv[:, SEG * (2 + i):SEG * (3 + i)], s_f_rk[:]))
                    V(lambda v, i=i: v.tensor_mul(
                        s_kb[:, SEG * i:SEG * (i + 1)],
                        s_khat[:, SEG * i:SEG * (i + 1)], s_f_b[:]))
                    V(lambda v, i=i: v.tensor_mul(
                        s_vb[:, SEG * i:SEG * (i + 1)],
                        s_qkv[:, SEG * (4 + i):SEG * (5 + i)], s_f_b[:]))

                wait("sync", ch_v, ch_t)
                for (src, dst) in ((s_khat, s_ktm), (s_vb, s_vtm)):
                    for ft in range(2):
                        def tr(e, src=src, dst=dst, ft=ft):
                            o3 = ap3(dst, 128 * ft, [[NCH * 256, 128], [256, NCH], [1, 128]])
                            i3 = src[:, SEG * ft:SEG * (ft + 1)].rearrange("p (c t) -> p c t", c=NCH)
                            return e.dma_start_transpose(o3, i3)
                        DMA("sync", ch_ld, tr)

                if seg == DEBUG_SEG and dbg:
                    V(lambda v: v.tensor_copy(s_f_tmp[:, 0:2], s_ema_st[:]))
                V(lambda v: v.tensor_scalar(s_f_g1m[:], s_f_g[:], -1.0, 1.0, ALU.mult, ALU.add))
                for i in range(2):
                    V(lambda v, i=i: v.tensor_mul(
                        s_uema[:, SEG * i:SEG * (i + 1)],
                        s_qkv[:, SEG * (4 + i):SEG * (5 + i)], s_f_g1m[:]))
                wait("tensor", ch_v)
                for ft in range(2):
                    for c in range(NCH):
                        T(lambda t, ft=ft, c=c: t.transpose(
                            ps_A[:, 512 * ft + 128 * c:512 * ft + 128 * (c + 1)],
                            s_uema[:, SEG * ft + 128 * c:SEG * ft + 128 * (c + 1)],
                            s_identf[:]))
                wait("vector", ch_t)
                def ev_uetm(v):
                    o3 = ap3(s_uetm, 0, [[NCH * 256, 128], [128, 2 * NCH], [1, 128]])
                    i3 = ap3(ps_A, 0, [[1024, 128], [128, 2 * NCH], [1, 128]])
                    return v.tensor_copy(o3, i3)
                events["vector"].pop() if False else None
                def ev_uetm2(v):
                    for_cols = None
                    return v.tensor_copy(
                        ap3(s_uetm, 0, [[NCH * 256, 128], [1, 1024]]),
                        ap3(ps_A, 0, [[1024, 128], [1, 1024]]))
                V(ev_uetm2)
                wait("scalar", ch_v, ch_sld)
                S(lambda s: s.activation(s_f_p[:], s_f_lc[:], AF.Exp))
                for c in range(NCH):
                    wait("vector", ch_s)
                    V(lambda v, c=c: v.tensor_scalar(s_mt[:], s_f_lc[:, 128 * c:128 * (c + 1)],
                                                     s_neglc[:, c:c + 1], 0.0,
                                                     ALU.add, ALU.min))
                    wait("scalar", ch_v)
                    S(lambda s, c=c: s.activation(s_mt[:], s_mt[:], AF.Exp))
                    wait("vector", ch_s)
                    V(lambda v, c=c: v.tensor_tensor(s_mtf[:, 128 * c:128 * (c + 1)],
                                                     s_mt[:], s_mUinc[:], ALU.mult))
                wait("tensor", ch_v)
                for c in range(NCH):
                    for dv in range(2):
                        T(lambda t, c=c, dv=dv: t.matmul(
                            ps_B[:, 256 * c + 128 * dv:256 * c + 128 * (dv + 1)],
                            s_uetm[:, 512 * dv + 128 * c:512 * dv + 128 * (c + 1)],
                            s_mtf[:, 128 * c:128 * (c + 1)], start=True, stop=True))
                wait("vector", ch_t)
                for c in range(NCH):
                    for dv in range(2):
                        def ema_t1(v, c=c, dv=dv):
                            cb = ap3(s_ema_st, dv, [[2, 128], [0, 128]])
                            return v.tensor_tensor(s_emtmp[:],
                                                   s_f_p[:, 128 * c:128 * (c + 1)], cb, ALU.mult)
                        V(ema_t1)
                        def ema_t2(v, c=c, dv=dv):
                            return v.tensor_add(
                                s_ema[:, SEG * dv + 128 * c:SEG * dv + 128 * (c + 1)],
                                s_emtmp[:],
                                ps_B[:, 256 * c + 128 * dv:256 * c + 128 * (dv + 1)])
                        V(ema_t2)
                    for dv in range(2):
                        V(lambda v, c=c, dv=dv: v.tensor_copy(
                            s_ema_st[:, dv:dv + 1],
                            s_ema[:, SEG * dv + 128 * (c + 1) - 1:SEG * dv + 128 * (c + 1)]))
                wait("tensor", ch_v)

                wait("tensor", ch_ld, ch_v)
                for c in range(NCH):
                    for ki in range(2):
                        T(lambda t, c=c, ki=ki: t.matmul(
                            ps_A[:, 128 * c:128 * (c + 1)],
                            s_kb[:, SEG * ki + 128 * c:SEG * ki + 128 * (c + 1)],
                            s_khat[:, SEG * ki + 128 * c:SEG * ki + 128 * (c + 1)],
                            start=(ki == 0), stop=(ki == 1)))
                    for ki in range(2):
                        T(lambda t, c=c, ki=ki: t.matmul(
                            ps_B[:, 128 * c:128 * (c + 1)],
                            s_khat[:, SEG * ki + 128 * c:SEG * ki + 128 * (c + 1)],
                            s_kb[:, SEG * ki + 128 * c:SEG * ki + 128 * (c + 1)],
                            start=(ki == 0), stop=(ki == 1)))
                wait("vector", ch_t)
                mask_bc = lambda m: ap3(m, 0, [[128, 128], [0, NCH], [1, 128]])
                psv = lambda ps: ps[:, 0:TW].rearrange("p (c t) -> p c t", c=NCH)
                sbv = lambda s: s[:].rearrange("p (c t) -> p c t", c=NCH)
                V(lambda v: v.tensor_tensor(sbv(s_A), psv(ps_A), mask_bc(s_mLneg), ALU.mult))
                V(lambda v: v.tensor_tensor(sbv(s_AT), psv(ps_B), mask_bc(s_mUneg), ALU.mult))
                wait("tensor", ch_v)
                for c in range(NCH):
                    for ki in range(2):
                        T(lambda t, c=c, ki=ki: t.matmul(
                            ps_A[:, 128 * c:128 * (c + 1)],
                            s_khat[:, SEG * ki + 128 * c:SEG * ki + 128 * (c + 1)],
                            s_qkv[:, SEG * ki + 128 * c:SEG * ki + 128 * (c + 1)],
                            start=(ki == 0), stop=(ki == 1)))
                wait("vector", ch_t)
                V(lambda v: v.tensor_tensor(sbv(s_aT), psv(ps_A), mask_bc(s_mUinc), ALU.mult))
                wait("tensor", ch_v)

                curX, curXT = s_A, s_AT
                for lvl in range(6):
                    dstX = s_X[lvl]
                    dstXT = s_XT[lvl % 2] if lvl < 5 else None
                    for c in range(NCH):
                        T(lambda t, c=c, curX=curX, curXT=curXT: t.matmul(
                            ps_A[:, 128 * c:128 * (c + 1)],
                            curXT[:, 128 * c:128 * (c + 1)],
                            curX[:, 128 * c:128 * (c + 1)], start=True, stop=True))
                    if dstXT is not None:
                        for c in range(NCH):
                            T(lambda t, c=c, curX=curX, curXT=curXT: t.matmul(
                                ps_B[:, 128 * c:128 * (c + 1)],
                                curX[:, 128 * c:128 * (c + 1)],
                                curXT[:, 128 * c:128 * (c + 1)], start=True, stop=True))
                    wait("vector", ch_t)
                    V(lambda v, dstX=dstX: v.tensor_copy(dstX[:], ps_A[:, 0:TW]))
                    if dstXT is not None:
                        V(lambda v, dstXT=dstXT: v.tensor_copy(dstXT[:], ps_B[:, 0:TW]))
                    wait("tensor", ch_v)
                    curX, curXT = dstX, dstXT
                V(lambda v: v.tensor_tensor(sbv(s_G), sbv(s_AT),
                                            ap3(s_ident, 0, [[128, 128], [0, NCH], [1, 128]]),
                                            ALU.add))
                wait("tensor", ch_v)
                for lvl in range(6):
                    Xk = s_X[lvl]
                    for c in range(NCH):
                        T(lambda t, c=c, Xk=Xk: t.matmul(
                            ps_A[:, 128 * c:128 * (c + 1)], Xk[:, 128 * c:128 * (c + 1)],
                            s_G[:, 128 * c:128 * (c + 1)], start=True, stop=False))
                        T(lambda t, c=c: t.matmul(
                            ps_A[:, 128 * c:128 * (c + 1)], s_ident[:],
                            s_G[:, 128 * c:128 * (c + 1)], start=False, stop=True))
                    wait("vector", ch_t)
                    V(lambda v: v.tensor_copy(s_G[:], ps_A[:, 0:TW]))
                    wait("tensor", ch_v)

                for c in range(NCH):
                    T(lambda t, c=c: t.matmul(ps_C[:, 0:256], s_kb[:, 128 * c:128 * (c + 1)],
                                              s_Sbf[:, 0:256], start=True, stop=False))
                    T(lambda t, c=c: t.matmul(ps_C[:, 0:256],
                                              s_kb[:, SEG + 128 * c:SEG + 128 * (c + 1)],
                                              s_Sbf[:, 256:512], start=False, stop=True))
                    wait("vector", ch_t)
                    V(lambda v, c=c: v.tensor_sub(s_r[:], s_vtm[:, 256 * c:256 * (c + 1)],
                                                  ps_C[:, 0:256]))
                    wait("tensor", ch_v)
                    T(lambda t, c=c: t.matmul(ps_D[:, 0:256], s_G[:, 128 * c:128 * (c + 1)],
                                              s_r[:], start=True, stop=True))
                    wait("scalar", ch_t)
                    S(lambda s: s.copy(s_u[:], ps_D[:, 0:256]))
                    wait("tensor", ch_s)
                    for dv in range(2):
                        T(lambda t, c=c, dv=dv: t.matmul(
                            ps_D[:, 256 + 128 * dv:384 + 128 * dv],
                            s_Sbf[:, 128 * dv:128 * (dv + 1)],
                            s_qkv[:, 128 * c:128 * (c + 1)], start=True, stop=False))
                        T(lambda t, c=c, dv=dv: t.matmul(
                            ps_D[:, 256 + 128 * dv:384 + 128 * dv],
                            s_Sbf[:, 256 + 128 * dv:384 + 128 * dv],
                            s_qkv[:, SEG + 128 * c:SEG + 128 * (c + 1)],
                            start=False, stop=False))
                        T(lambda t, c=c, dv=dv: t.matmul(
                            ps_D[:, 256 + 128 * dv:384 + 128 * dv],
                            s_u[:, 128 * dv:128 * (dv + 1)],
                            s_aT[:, 128 * c:128 * (c + 1)], start=False, stop=True))
                    wait("vector", ch_t)
                    def ev_o(v, c=c):
                        o_ap = ap3(s_oseg, 128 * c, [[2 * SEG, 128], [SEG, 2], [1, 128]])
                        ps_view = ap3(ps_D, 256, [[512, 128], [128, 2], [1, 128]])
                        rq3 = ap3(s_f_rq, 128 * c, [[SEG, 128], [0, 2], [1, 128]])
                        return v.tensor_tensor(o_ap, ps_view, rq3, ALU.mult)
                    V(ev_o)
                    T(lambda t, c=c: t.matmul(ps_C[:, 0:256], s_ktm[:, 256 * c:256 * c + 128],
                                              s_u[:], start=True, stop=True))
                    T(lambda t, c=c: t.matmul(ps_C[:, 256:512],
                                              s_ktm[:, 256 * c + 128:256 * (c + 1)],
                                              s_u[:], start=True, stop=True))
                    wait("vector", ch_t)
                    V(lambda v: v.tensor_add(s_S[:], s_S[:], ps_C[:]))
                    wait("scalar", ch_v)
                    S(lambda s: s.copy(s_Sbf[:], s_S[:]))
                    wait("tensor", ch_s)

                for i in range(2):
                    V(lambda v, i=i: v.tensor_sub(s_uema[:, SEG * i:SEG * (i + 1)],
                                                  s_ema[:, SEG * i:SEG * (i + 1)],
                                                  s_oseg[:, SEG * i:SEG * (i + 1)]))
                    V(lambda v, i=i: v.tensor_mul(s_uema[:, SEG * i:SEG * (i + 1)],
                                                  s_uema[:, SEG * i:SEG * (i + 1)], s_f_mix[:]))
                    V(lambda v, i=i: v.tensor_add(s_oseg[:, SEG * i:SEG * (i + 1)],
                                                  s_oseg[:, SEG * i:SEG * (i + 1)],
                                                  s_uema[:, SEG * i:SEG * (i + 1)]))
                wait("scalar", ch_v)
                wait("scalar", ch_t)
                for i in range(2):
                    buf = s_sq if i % 2 == 0 else s_sqB
                    S(lambda s, i=i, buf=buf: s.activation(buf[:], s_oseg[:, SEG * i:SEG * (i + 1)],
                                                           AF.Square))
                    wait("tensor", ch_s)
                    T(lambda t, i=i, buf=buf: t.matmul(ps_F[0:1, 0:SEG], s_ones[:], buf[:],
                                                       start=(i == 0), stop=(i == 1)))
                wait("vector", ch_t, ch_sld)
                V(lambda v: v.tensor_copy(s_row_a[0:1, :], ps_F[0:1, 0:SEG]))
                wait("scalar", ch_v)
                row_store(seg, 7, s_row_a)
                wait("scalar", ch_sld)
                row_bcast(seg, 7, s_f_rms)
                wait("scalar", ch_sld)
                S(lambda s: s.activation(s_f_rms[:], s_f_rms[:], AF.Ln,
                                         bias=cst(C_EPS_RMS), scale=cst(C_I256)))
                S(lambda s: s.activation(s_f_rms[:], s_f_rms[:], AF.Exp, scale=cst(C_NHALF)))
                wait("vector", ch_s)
                for i in range(2):
                    V(lambda v, i=i: v.tensor_mul(s_oseg[:, SEG * i:SEG * (i + 1)],
                                                  s_oseg[:, SEG * i:SEG * (i + 1)], s_f_rms[:]))
                    V(lambda v, i=i: v.tensor_scalar(s_obf[:, SEG * i:SEG * (i + 1)],
                                                     s_oseg[:, SEG * i:SEG * (i + 1)],
                                                     s_onorm[:, i:i + 1], None, ALU.mult))

                wait("gpsimd", ch_v)
                DMA("gpsimd", ch_gld, lambda e: e.dma_start(
                    out=ap3(cc_in.tensor, 0, [[SEG, 128], [128 * SEG, 2], [1, SEG]]),
                    in_=s_obf[:].rearrange("p (d t) -> p d t", d=2)))
                wait("gpsimd", ch_gld)
                CC(lambda g: g.collective_compute(
                    "AllGather", ALU.bypass,
                    replica_groups=[[0, 1, 2, 3], [4, 5, 6, 7]],
                    ins=[cc_in.opt()], outs=[cc_out.opt()]))
                wait("gpsimd", ch_cc)
                DMA("gpsimd", ch_gld, lambda e: e.dma_start(
                    out=s_oag[:].rearrange("p (f t) -> p f t", f=8),
                    in_=ap3(cc_out.tensor, 0, [[SEG, 128], [128 * SEG, 8], [1, SEG]])))

                wait("vector", ch_gld, ch_ld)
                V(lambda v: v.tensor_scalar(
                    s_ocat[:].rearrange("p (f t) -> p f t", f=8),
                    ap3(s_oag, 0, [[8 * SEG, 128], [SEG, 8], [1, QT]]),
                    s_msel[:, 0:1], None, ALU.mult))
                for p in range(1, 4):
                    V(lambda v, p=p: v.scalar_tensor_tensor(
                        s_ocat[:].rearrange("p (f t) -> p f t", f=8),
                        ap3(s_oag, QT * p, [[8 * SEG, 128], [SEG, 8], [1, QT]]),
                        s_msel[:, p:p + 1],
                        s_ocat[:].rearrange("p (f t) -> p f t", f=8),
                        ALU.mult, ALU.add))
                wait("tensor", ch_v)
                pevac = [0, 0]
                for ot in range(8):
                    tgt = ps_E if (ot % 2 == 0) else ps_F
                    wait_at("tensor", ch_v, pevac[ot % 2])
                    for kt in range(8):
                        T(lambda t, ot=ot, kt=kt, tgt=tgt: t.matmul(
                            tgt[:, 0:QT],
                            s_wot[:, 1024 * kt + 128 * ot:1024 * kt + 128 * (ot + 1)],
                            s_ocat[:, QT * kt:QT * (kt + 1)],
                            start=(kt == 0), stop=(kt == 7)))
                    wait("vector", ch_t)
                    V(lambda v, ot=ot, tgt=tgt: v.tensor_copy(
                        s_attn[:, QT * ot:QT * (ot + 1)], tgt[:, 0:QT]))
                    pevac[ot % 2] = ch_v.n
                V(lambda v: v.tensor_copy(s_abf[:], s_attn[:]))
                wait("tensor", ch_v)
                for ot in range(8):
                    T(lambda t, ot=ot: t.matmul(ps_E[0:1, 0:QT], s_ones[:],
                                                s_abf[:, QT * ot:QT * (ot + 1)],
                                                start=(ot == 0), stop=(ot == 7)))
                wait("scalar", ch_v)
                mm_done = [0, 0]
                for ot in range(8):
                    buf = s_sq if ot % 2 == 0 else s_sqB
                    wait_at("scalar", ch_t, mm_done[ot % 2])
                    S(lambda s, ot=ot, buf=buf: s.activation(
                        buf[:, 0:QT], s_abf[:, QT * ot:QT * (ot + 1)], AF.Square))
                    wait("tensor", ch_s)
                    T(lambda t, ot=ot, buf=buf: t.matmul(
                        ps_F[0:1, 0:QT], s_ones[:], buf[:, 0:QT],
                        start=(ot == 0), stop=(ot == 7)))
                    mm_done[ot % 2] = ch_t.n
                wait("vector", ch_t, ch_sld)
                V(lambda v: v.tensor_copy(s_row_a[0:1, 0:QT], ps_E[0:1, 0:QT]))
                V(lambda v: v.tensor_copy(s_row_b[0:1, 0:QT], ps_F[0:1, 0:QT]))
                wait("scalar", ch_v)
                row_store(seg, 8, s_row_a, w=QT)
                row_store(seg, 9, s_row_b, w=QT)
                wait("scalar", ch_sld)
                row_bcast(seg, 8, s_fq_a, w=QT)
                row_bcast(seg, 9, s_fq_b, w=QT)
                ln_fields(s_fq_a, s_fq_b, s_fq_c, w=QT)
                if seg == DEBUG_SEG and dbg:
                    wait("vector", ch_s)
                    V(lambda v: v.tensor_copy(s_f_g1m[:, 0:QT], s_fq_a[:, 0:QT]))
                    V(lambda v: v.tensor_copy(s_f_g1m[:, QT:2 * QT], s_fq_b[:, 0:QT]))
                DMA("sync", ch_ld, lambda e, seg=seg: e.dma_start(
                    out=s_xres[:].rearrange("p (f t) -> p f t", f=8),
                    in_=ap3(x_mine.tensor, QT * seg, [[MYT, 128], [MYT * 128, 8], [1, QT]])))
                wait("vector", ch_s, ch_ld)
                for ot in range(8):
                    V(lambda v, ot=ot: v.tensor_sub(s_x1[:, QT * ot:QT * (ot + 1)],
                                                    s_attn[:, QT * ot:QT * (ot + 1)], s_fq_a[:]))
                    V(lambda v, ot=ot: v.tensor_mul(s_x1[:, QT * ot:QT * (ot + 1)],
                                                    s_x1[:, QT * ot:QT * (ot + 1)], s_fq_b[:]))
                    V(lambda v, ot=ot: v.tensor_add(s_x1[:, QT * ot:QT * (ot + 1)],
                                                    s_x1[:, QT * ot:QT * (ot + 1)],
                                                    s_xres[:, QT * ot:QT * (ot + 1)]))
                wait("sync", ch_v)
                DMA("sync", ch_ld, lambda e, seg=seg: e.dma_start(
                    out=ap3(x1_dram.tensor, QT * seg, [[MYT, 128], [MYT * 128, 8], [1, QT]]),
                    in_=s_x1[:].rearrange("p (f t) -> p f t", f=8)))
                V(lambda v: v.tensor_copy(s_abf[:], s_x1[:]))
                wait("tensor", ch_v)
                for ot in range(8):
                    T(lambda t, ot=ot: t.matmul(ps_E[0:1, 0:QT], s_ones[:],
                                                s_abf[:, QT * ot:QT * (ot + 1)],
                                                start=(ot == 0), stop=(ot == 7)))
                wait("scalar", ch_v)
                mm_done = [0, 0]
                for ot in range(8):
                    buf = s_sq if ot % 2 == 0 else s_sqB
                    wait_at("scalar", ch_t, mm_done[ot % 2])
                    S(lambda s, ot=ot, buf=buf: s.activation(
                        buf[:, 0:QT], s_abf[:, QT * ot:QT * (ot + 1)], AF.Square))
                    wait("tensor", ch_s)
                    T(lambda t, ot=ot, buf=buf: t.matmul(
                        ps_F[0:1, 0:QT], s_ones[:], buf[:, 0:QT],
                        start=(ot == 0), stop=(ot == 7)))
                    mm_done[ot % 2] = ch_t.n
                wait("vector", ch_t, ch_sld)
                V(lambda v: v.tensor_copy(s_row_a[0:1, 0:QT], ps_E[0:1, 0:QT]))
                V(lambda v: v.tensor_copy(s_row_b[0:1, 0:QT], ps_F[0:1, 0:QT]))
                wait("scalar", ch_v)
                row_store(seg, 10, s_row_a, w=QT)
                row_store(seg, 11, s_row_b, w=QT)
                wait("scalar", ch_sld, ch_v)
                row_bcast(seg, 10, s_fq_a, w=QT)
                row_bcast(seg, 11, s_fq_b, w=QT)
                ln_fields(s_fq_a, s_fq_b, s_fq_c, w=QT)
                wait("vector", ch_s, ch_ld)
                for ot in range(8):
                    V(lambda v, ot=ot: v.tensor_sub(s_x1[:, QT * ot:QT * (ot + 1)],
                                                    s_x1[:, QT * ot:QT * (ot + 1)], s_fq_a[:]))
                    V(lambda v, ot=ot: v.tensor_mul(s_abf[:, QT * ot:QT * (ot + 1)],
                                                    s_x1[:, QT * ot:QT * (ot + 1)], s_fq_b[:]))
                wait("sync", ch_v)
                DMA("sync", ch_ld, lambda e, seg=seg: e.dma_start(
                    out=ap3(ln4_dram.tensor, QT * seg, [[MYT, 128], [MYT * 128, 8], [1, QT]]),
                    in_=s_abf[:].rearrange("p (f t) -> p f t", f=8)))
                if seg == DEBUG_SEG:
                    wait("gpsimd", ch_t, ch_v, ch_s, ch_ld, ch_sld)
                    for nm, src, w in (("obf", s_obf, 2 * SEG), ("ocat", s_ocat, 8 * QT),
                                       ("attn", s_attn, 8 * QT), ("x1", s_x1, 8 * QT),
                                       ("G", s_G, TW), ("vtm", s_vtm, NCH * 256),
                                       ("ktm", s_ktm, NCH * 256), ("oag", s_oag, 8 * SEG),
                                       ("qkv", s_qkv, 6 * SEG), ("lnx", s_lnx, 8 * SEG),
                                       ("ema", s_ema, 2 * SEG), ("frq", s_f_rq, SEG),
                                       ("aT", s_aT, TW), ("emin", s_f_tmp, 2),
                                       ("ln2f", s_f_g1m, 2 * QT),
                                       ("uem", s_uema, 2 * SEG), ("fg", s_f_g, SEG)):
                        DMA("gpsimd", ch_gld, lambda e, nm=nm, src=src, w=w: e.dma_start(
                            out=ap3(dbg[nm].tensor, 0, [[8 * SEG, 128], [1, w]]),
                            in_=src[:, 0:w]))
                    wait("gpsimd", ch_gld)

            wait("tensor")
            wait("vector")
            wait("scalar")
            wait("sync")
            wait("gpsimd")
            flush_events()

        with ExitStack() as c4:
            s_ln4 = sbuf("s_ln4", [128, 8 * PBT], c=c4)
            s_z = sbuf("s_z", [128, 16 * PBT], c=c4)
            s_wc = sbuf("s_wc", [128, 16 * 1024], c=c4)
            s_acc = sbuf("s_acc", [128, 8 * MYT], F32, c=c4)
            s_sq2 = sbuf("s_sq2", [128, 512], c=c4)
            s_fb_a = sbuf("s_fb_a", [128, PBT], F32, c=c4)
            s_fb_b = sbuf("s_fb_b", [128, PBT], F32, c=c4)
            s_fb_c = sbuf("s_fb_c", [128, PBT], F32, c=c4)
            s_x2 = sbuf("s_x2", [128, PBT], F32, c=c4)
            s_xt1 = sbuf("s_xt1", [128, PBT], F32, c=c4)

            ps_z0 = c4.enter_context(nc.psum_tensor("ps_z0", [128, 1024], F32))
            ps_z1 = c4.enter_context(nc.psum_tensor("ps_z1", [128, 1024], F32))
            ps_c0 = c4.enter_context(nc.psum_tensor("ps_c0", [128, 512], F32))
            ps_c1 = c4.enter_context(nc.psum_tensor("ps_c1", [128, 512], F32))
            ps_r = c4.enter_context(nc.psum_tensor("ps_r", [128, 512], F32))

            ln4v = s_ln4[:].rearrange("p (f t) -> p f t", f=8)

            for e8 in range(8):
                wait("gpsimd", ch_t, ch_v, ch_s)
                wait("tensor", ch_v, ch_s)
                DMA("gpsimd", ch_gld, lambda e, e8=e8: e.dma_start(
                    out=s_wc[:].rearrange("p (k c) -> p k c", k=16),
                    in_=ap3(wct.tensor, 2048 * 1024 * e8,
                            [[1024, 128], [1024 * 128, 16], [1, 1024]])))
                for bi in range(2):
                    tb0 = PBT * bi
                    wait("sync", ch_t)
                    DMA("sync", ch_ld, lambda e, tb0=tb0: e.dma_start(
                        out=ln4v,
                        in_=ap3(ln4_dram.tensor, tb0, [[MYT, 128], [MYT * 128, 8], [1, PBT]])))
                    wait("tensor", ch_ld, ch_v)
                    gelu_done = [0, 0]
                    for zi in range(16):
                        zt = 16 * e8 + zi
                        bp = 32 * ((zt % 32) % 4)
                        ft = (zt % 32) // 4
                        tgt = ps_z0 if (zi % 2 == 0) else ps_z1
                        wait_at("tensor", ch_s, gelu_done[zi % 2])
                        for th in range(2):
                            T(lambda t, zt=zt, bp=bp, ft=ft, th=th, tgt=tgt, tb0=tb0: t.matmul(
                                tgt[:, 512 * th:512 * (th + 1)],
                                s_compT[bp:bp + 32,
                                        128 * ((zt % 32) // 4) + 1024 * (zt // 32):
                                        128 * ((zt % 32) // 4) + 1024 * (zt // 32) + 128],
                                ln4v[bp:bp + 32, ft, 512 * th:512 * (th + 1)],
                                start=True, stop=True, tile_position=(bp, 0)))
                        wait("scalar", ch_t)
                        S(lambda s, zi=zi, tgt=tgt: s.activation(
                            s_z[:, PBT * zi:PBT * (zi + 1)], tgt[:, 0:PBT], AF.Gelu))
                        gelu_done[zi % 2] = ch_s.n
                    wait("tensor", ch_s)
                    wait("tensor", ch_gld)
                    evac_done = [0, 0]
                    for ot in range(8):
                        for th in range(2):
                            par = (ot * 2 + th) % 2
                            tgt = ps_c0 if par == 0 else ps_c1
                            wait_at("tensor", ch_v, evac_done[par])
                            for kt in range(16):
                                T(lambda t, ot=ot, th=th, kt=kt, tgt=tgt: t.matmul(
                                    tgt[:, 0:512],
                                    s_wc[:, 1024 * kt + 128 * ot:1024 * kt + 128 * (ot + 1)],
                                    s_z[:, PBT * kt + 512 * th:PBT * kt + 512 * (th + 1)],
                                    start=(kt == 0), stop=(kt == 15)))
                            wait("vector", ch_t)
                            col = MYT * ot + tb0 + 512 * th
                            if e8 == 0:
                                V(lambda v, col=col, tgt=tgt: v.tensor_copy(
                                    s_acc[:, col:col + 512], tgt[:, 0:512]))
                            else:
                                V(lambda v, col=col, tgt=tgt: v.tensor_add(
                                    s_acc[:, col:col + 512], s_acc[:, col:col + 512],
                                    tgt[:, 0:512]))
                            evac_done[par] = ch_v.n
                    wait("tensor", ch_v)


            for bi in range(2):
                tb0 = PBT * bi
                for th in range(2):
                    for ot in range(8):
                        col = MYT * ot + tb0 + 512 * th
                        wait("vector", ch_t, ch_s)
                        V(lambda v, col=col: v.tensor_copy(s_sq2[:], s_acc[:, col:col + 512]))
                        wait("scalar", ch_v)
                        S(lambda s: s.activation(s_sq[:, 0:512], s_sq2[:], AF.Square))
                        wait("tensor", ch_v, ch_s)
                        T(lambda t, ot=ot: t.matmul(ps_r[0:1, 0:512], s_ones[:], s_sq2[:],
                                                    start=(ot == 0), stop=(ot == 7)))
                        T(lambda t, ot=ot: t.matmul(ps_c0[0:1, 0:512], s_ones[:], s_sq[:, 0:512],
                                                    start=(ot == 0), stop=(ot == 7)))
                    wait("vector", ch_t, ch_sld)
                    V(lambda v: v.tensor_copy(s_row_a[0:1, 0:512], ps_r[0:1, 0:512]))
                    V(lambda v: v.tensor_copy(s_row_b[0:1, 0:512], ps_c0[0:1, 0:512]))
                    wait("scalar", ch_v)
                    DMA("scalar", ch_sld, lambda e, bi=bi, th=th: e.dma_start(
                        out=ap3(rowbuf2.tensor, (bi * 4 + 0) * PBT + 512 * th, [[1, 512]]),
                        in_=s_row_a[0:1, 0:512]))
                    DMA("scalar", ch_sld, lambda e, bi=bi, th=th: e.dma_start(
                        out=ap3(rowbuf2.tensor, (bi * 4 + 1) * PBT + 512 * th, [[1, 512]]),
                        in_=s_row_b[0:1, 0:512]))
                wait("scalar", ch_sld, ch_v)
                DMA("scalar", ch_sld, lambda e, bi=bi: e.dma_start(
                    out=s_fb_a[:],
                    in_=ap3(rowbuf2.tensor, (bi * 4 + 0) * PBT, [[0, 128], [1, PBT]])))
                DMA("scalar", ch_sld, lambda e, bi=bi: e.dma_start(
                    out=s_fb_b[:],
                    in_=ap3(rowbuf2.tensor, (bi * 4 + 1) * PBT, [[0, 128], [1, PBT]])))
                wait("vector", ch_sld)
                V(lambda v: v.tensor_scalar(s_fb_a[:], s_fb_a[:], cst(C_I1024), None, ALU.mult))
                V(lambda v: v.tensor_scalar(s_fb_b[:], s_fb_b[:], cst(C_I1024), None, ALU.mult))
                wait("scalar", ch_v)
                S(lambda s: s.activation(s_fb_c[:], s_fb_a[:], AF.Square))
                wait("vector", ch_s)
                V(lambda v: v.tensor_sub(s_fb_b[:], s_fb_b[:], s_fb_c[:]))
                wait("scalar", ch_v)
                S(lambda s: s.activation(s_fb_b[:], s_fb_b[:], AF.Ln, bias=cst(C_EPS_LN)))
                S(lambda s: s.activation(s_fb_b[:], s_fb_b[:], AF.Exp, scale=cst(C_NHALF)))
                for ot in range(8):
                    col = MYT * ot + tb0
                    wait("sync", ch_v)
                    DMA("sync", ch_ld, lambda e, ot=ot, tb0=tb0: e.dma_start(
                        out=s_xt1[:],
                        in_=ap3(x1_dram.tensor, MYT * 128 * ot + tb0, [[MYT, 128], [1, PBT]])))
                    wait("vector", ch_s, ch_ld)
                    V(lambda v, col=col: v.tensor_sub(s_x2[:], s_acc[:, col:col + PBT], s_fb_a[:]))
                    V(lambda v: v.tensor_mul(s_x2[:], s_x2[:], s_fb_b[:]))
                    V(lambda v: v.tensor_add(s_x2[:], s_x2[:], s_xt1[:]))
                    wait("sync", ch_v)
                    DMA("sync", ch_ld, lambda e, ot=ot, tb0=tb0: e.dma_start(
                        out=ap3(out.tensor, MYT * 128 * ot + tb0, [[MYT, 128], [1, PBT]]),
                        in_=s_x2[:]))
            wait("sync")
            wait("gpsimd")
            flush_events()

    return nc


_NC = None
_LAST = None


def _get_nc():
    global _NC
    if _NC is None:
        _NC = build_program()
    return _NC


def kernel(**inputs):
    f32 = np.float32
    hs = np.asarray(inputs["L0_attn_hidden_states"], f32)
    Wq = np.asarray(inputs["L0_attn_Wq"], f32)
    Wk = np.asarray(inputs["L0_attn_Wk"], f32)
    Wv = np.asarray(inputs["L0_attn_Wv"], f32)
    Wb = np.asarray(inputs["L0_attn_Wb"], f32)
    Wdec = np.asarray(inputs["L0_attn_Wdec"], f32)
    Wmix = np.asarray(inputs["L0_attn_Wmix"], f32)
    mix_bias = np.asarray(inputs["L0_attn_mix_bias"], f32)
    wq_conv = np.asarray(inputs["L0_attn_wq_conv"], f32)
    wk_conv = np.asarray(inputs["L0_attn_wk_conv"], f32)
    wv_conv = np.asarray(inputs["L0_attn_wv_conv"], f32)
    onorm_w = np.asarray(inputs["L0_attn_onorm_w"], f32)
    Wo = np.asarray(inputs["L0_attn_Wo"], f32)
    maker = np.asarray(inputs["L0_ffn_rail_maker"], f32)
    up = np.asarray(inputs["L0_ffn_rail_up_proj"], f32)
    Wc = np.asarray(inputs["L0_ffn_w_cproj"], f32)

    consts = np.tile(np.array([1e-5, 1e-6, 1e-5, -0.5, 1.0 / 1024.0, 1.0 / 256.0, 1.0, 0.0],
                              f32), (128, 1))
    wct = np.ascontiguousarray(Wc.T)
    wot = np.ascontiguousarray(Wo.T)

    mk_bd = np.zeros((32, 128, 128), f32)
    up_bd = np.zeros((32, 128, 512), f32)
    for g in range(32):
        hh, rg = g // 8, g % 8
        for rr in range(8):
            r = 8 * rg + rr
            mk_bd[g, 16 * rr:16 * rr + 16, 16 * rr:16 * rr + 16] = maker[hh, r].T
            up_bd[g, 16 * rr:16 * rr + 16, 64 * rr:64 * rr + 64] = up[hh, r].T

    x_pad = [None, None]
    for b in range(2):
        xt = np.ascontiguousarray(hs[b].T)
        x_pad[b] = np.ascontiguousarray(
            np.concatenate([np.zeros((D, 3), f32), xt], axis=1))

    in_maps = []
    for c in range(8):
        b, h = c // 4, c % 4
        xt = x_pad[b][:, 3:]
        x_mine = np.ascontiguousarray(np.concatenate(
            [xt[:, SEG * s + QT * h:SEG * s + QT * h + QT] for s in range(NSEG)], axis=1))
        sl = slice(DK * h, DK * (h + 1))
        wp = np.ascontiguousarray(np.concatenate(
            [Wq.T[:, sl], Wk.T[:, sl], Wv.T[:, sl],
             Wb.T[:, h:h + 1], Wdec.T[:, h:h + 1], Wmix.T[:, h:h + 1]], axis=1))
        cw = np.ascontiguousarray(np.concatenate(
            [wq_conv[sl], wk_conv[sl], wv_conv[sl]], axis=0))
        ms = np.zeros((128, 4), f32)
        ms[:, h] = 1.0
        in_maps.append({
            "x_in": x_pad[b],
            "x_mine": x_mine,
            "wproj": wp,
            "convw": cw,
            "wot": wot,
            "wct": wct,
            "mk_bd": mk_bd,
            "up_bd": up_bd,
            "onormw": np.ascontiguousarray(onorm_w.reshape(2, 128).T),
            "mixb": mix_bias[h].reshape(1, 1).astype(f32),
            "consts": consts,
            "msel": ms,
        })

    nc = _get_nc()
    res = run_bass_kernel_spmd(nc, in_maps, core_ids=list(range(8)))
    global _LAST
    _LAST = res

    out_full = np.empty((2, L, D), f32)
    for c in range(8):
        b, h = c // 4, c % 4
        sh = res.results[c]["out"]
        sh = np.asarray(sh, f32).T.reshape(NSEG, QT, D)
        out_full[b].reshape(NSEG, 4, QT, D)[:, h] = sh
    return out_full


# version 06 (final)
# speedup vs baseline: 1.2308x; 1.1140x over previous
import numpy as np
import ml_dtypes
import concourse.bass as bass
import concourse.mybir as mybir
from concourse.bass_utils import run_bass_kernel_spmd
from contextlib import ExitStack

F32 = mybir.dt.float32
BF16 = mybir.dt.bfloat16
AF = mybir.ActivationFunctionType
ALU = mybir.AluOpType

B, L, D, H = 2, 8192, 1024, 4
DK = 256
CH = 128
SEG = 512
NSEG = L // SEG
NCH = SEG // CH
QT = SEG // 4
MYT = L // 4
PBT = 1024
TW = NCH * CH

C_EPS_LN, C_EPS_L2, C_EPS_RMS, C_NHALF, C_I1024, C_I256, C_ONE, C_ZERO = range(8)

DEBUG_SEG = -1
DEBUG_ALL = False


def build_program():
    nc = bass.Bass("TRN2", target_bir_lowering=False)

    x_in = nc.dram_tensor("x_in", [D, L + 3], F32, kind="ExternalInput").ap()
    x_mine = nc.dram_tensor("x_mine", [D, MYT], F32, kind="ExternalInput").ap()
    wproj = nc.dram_tensor("wproj", [D, 833], F32, kind="ExternalInput").ap()
    convw = nc.dram_tensor("convw", [3 * DK, 4], F32, kind="ExternalInput").ap()
    wot = nc.dram_tensor("wot", [D, D], F32, kind="ExternalInput").ap()
    wct = nc.dram_tensor("wct", [16 * D, D], F32, kind="ExternalInput").ap()
    mk_bd = nc.dram_tensor("mk_bd", [32, 128, 128], F32, kind="ExternalInput").ap()
    up_bd = nc.dram_tensor("up_bd", [32, 128, 512], F32, kind="ExternalInput").ap()
    onormw = nc.dram_tensor("onormw", [128, 2], F32, kind="ExternalInput").ap()
    mixb = nc.dram_tensor("mixb", [1, 1], F32, kind="ExternalInput").ap()
    consts = nc.dram_tensor("consts", [128, 8], F32, kind="ExternalInput").ap()
    msel = nc.dram_tensor("msel", [128, 4], F32, kind="ExternalInput").ap()
    out = nc.dram_tensor("out", [D, MYT], F32, kind="ExternalOutput").ap()

    rowbuf = nc.dram_tensor("rowbuf", [NSEG, 16, SEG], F32).ap()
    rowbuf2 = nc.dram_tensor("rowbuf2", [2, 4, PBT], F32).ap()
    x1_dram = nc.dram_tensor("x1_dram", [D, MYT], F32).ap()
    ln4_dram = nc.dram_tensor("ln4_dram", [D, MYT], BF16).ap()
    cc_in = nc.dram_tensor("cc_in", [2, 128, SEG], BF16).ap()
    cc_out = nc.dram_tensor("cc_out", [4, 2, 128, SEG], BF16).ap()

    dbgA = {}
    if DEBUG_ALL:
        dbgA["obf"] = nc.dram_tensor("dba_obf", [128, NSEG * 1024], BF16,
                                     kind="ExternalOutput").ap()
        dbgA["oag"] = nc.dram_tensor("dba_oag", [128, NSEG * 4096], BF16,
                                     kind="ExternalOutput").ap()
        dbgA["ocat"] = nc.dram_tensor("dba_ocat", [128, NSEG * 1024], BF16,
                                      kind="ExternalOutput").ap()
    dbg = {}
    if DEBUG_SEG >= 0:
        for nm in ("obf", "ocat", "attn", "x1", "G", "vtm", "ktm", "oag", "qkv", "lnx",
                   "ema", "frq", "aT", "emin", "uem", "fg", "ln2f"):
            dbg[nm] = nc.dram_tensor("dbg_" + nm, [128, 8 * SEG], F32, kind="ExternalOutput").ap()

    events = {"tensor": [], "vector": [], "scalar": [], "gpsimd": [], "sync": []}
    ap3 = lambda h, off, dims: bass.AP(h, off, dims)

    with ExitStack() as ctx:
        def sbuf(name, shape, dt=BF16, c=None):
            return (c or ctx).enter_context(nc.sbuf_tensor(name, shape, dt))

        class Chan:
            def __init__(self, name):
                self.sem = ctx.enter_context(nc.semaphore(name))
                self.n = 0

        ch_ld = Chan("ch_ld")
        ch_sld = Chan("ch_sld")
        ch_gld = Chan("ch_gld")
        ch_t = Chan("ch_t")
        ch_v = Chan("ch_v")
        ch_s = Chan("ch_s")
        ch_cc = Chan("ch_cc")
        ch_g = Chan("ch_g")
        CHANNELS = [ch_ld, ch_sld, ch_gld, ch_t, ch_v, ch_s, ch_cc, ch_g]

        def wait_at(eng_name, ch, n):
            if n > 0:
                def fn(eng, sem=ch.sem, n=n):
                    eng.wait_ge(sem, n)
                events[eng_name].append(fn)

        def wait(eng_name, *chs):
            if not chs:
                chs = CHANNELS
            snap = [(c.sem, c.n) for c in chs if c.n > 0]

            def fn(eng, snap=snap):
                for sem, n in snap:
                    eng.wait_ge(sem, n)
            events[eng_name].append(fn)

        def _mk(eng_name, ch):
            def emit(fn):
                def g(e, fn=fn):
                    fn(e).then_inc(ch.sem, 1)
                ch.n += 1
                events[eng_name].append(g)
            return emit

        T = _mk("tensor", ch_t)
        V = _mk("vector", ch_v)
        S = _mk("scalar", ch_s)
        G = _mk("gpsimd", ch_g)
        CC = _mk("gpsimd", ch_cc)

        def DMA(eng_name, ch, fn):
            def g(e, fn=fn, ch=ch):
                fn(e).then_inc(ch.sem, 16)
            ch.n += 16
            events[eng_name].append(g)

        def flush_events():
            with nc.Block() as block:
                for name in ("sync", "gpsimd", "scalar", "vector", "tensor"):
                    evs = tuple(events[name])
                    events[name] = []
                    if evs:
                        getattr(block, name)(lambda eng, evs=evs: [fn(eng) for fn in evs])

        s_convw = sbuf("s_convw", [128, 24], F32)
        s_compT = sbuf("s_compT", [128, 32 * 128])
        s_onorm = sbuf("s_onorm", [128, 2], F32)
        s_consts = sbuf("s_consts", [128, 8], F32)
        s_msel = sbuf("s_msel", [128, 4], F32)
        s_mixbias = sbuf("s_mixbias", [65, 1], F32)
        s_ones = sbuf("s_ones", [128, 1])
        s_ones2 = sbuf("s_ones2", [128, 128])
        s_ones2f = sbuf("s_ones2f", [128, 128], F32)
        s_ident = sbuf("s_ident", [128, 128])
        s_identf = sbuf("s_identf", [128, 128], F32)
        s_mLneg = sbuf("s_mLneg", [128, 128])
        s_mUneg = sbuf("s_mUneg", [128, 128])
        s_mUinc = sbuf("s_mUinc", [128, 128])
        s_S = sbuf("s_S", [128, 512], F32)
        s_Sbf = sbuf("s_Sbf", [128, 512])
        s_ema_st = sbuf("s_ema_st", [128, 2], F32)
        s_one1 = sbuf("s_one1", [1, 1], F32)
        s_onef1 = sbuf("s_onef1", [33, 1], F32)
        s_row_a = sbuf("s_row_a", [1, SEG])
        s_row_b = sbuf("s_row_b", [1, SEG])
        s_row65 = sbuf("s_row65", [65, SEG], F32)
        s_row65b = sbuf("s_row65b", [65, SEG])
        s_sq = sbuf("s_sq", [128, SEG])
        s_sqB = sbuf("s_sqB", [128, SEG])
        s_sqC = sbuf("s_sqC", [128, QT])
        s_sqD = sbuf("s_sqD", [128, QT])
        s_row_c = sbuf("s_row_c", [1, QT])
        s_row_d = sbuf("s_row_d", [1, QT])

        cst = lambda j: s_consts[:, j:j + 1]

        def prelude(g):
            g.memset(s_ident[:], 1.0)
            g.affine_select(s_ident[:], s_ident[:], [[-1, 128]], ALU.is_equal, 0.0,
                            base=0, channel_multiplier=1)
            g.memset(s_identf[:], 1.0)
            g.affine_select(s_identf[:], s_identf[:], [[-1, 128]], ALU.is_equal, 0.0,
                            base=0, channel_multiplier=1)
            g.memset(s_mLneg[:], -1.0)
            g.affine_select(s_mLneg[:], s_mLneg[:], [[-1, 128]], ALU.is_gt, 0.0,
                            base=0, channel_multiplier=1)
            g.memset(s_mUneg[:], -1.0)
            g.affine_select(s_mUneg[:], s_mUneg[:], [[1, 128]], ALU.is_gt, 0.0,
                            base=0, channel_multiplier=-1)
            g.memset(s_mUinc[:], 1.0)
            g.affine_select(s_mUinc[:], s_mUinc[:], [[1, 128]], ALU.is_ge, 0.0,
                            base=0, channel_multiplier=-1)
            g.memset(s_ones[:], 1.0)
            g.memset(s_one1[:], 1.0)
            g.memset(s_onef1[:], 1.0)
            g.memset(s_ones2[:], 1.0)
            g.memset(s_ones2f[:], 1.0)
            g.memset(s_S[:], 0.0)
            g.memset(s_Sbf[:], 0.0)
            g.memset(s_ema_st[:], 0.0)
            return g.memset(s_mixbias[:], 0.0)
        G(prelude)

        DMA("sync", ch_ld, lambda e: e.dma_start(out=s_consts[:], in_=consts))
        DMA("sync", ch_ld, lambda e: e.dma_start(out=s_msel[:], in_=msel))
        DMA("sync", ch_ld, lambda e: e.dma_start(
            out=s_convw[:].rearrange("p (t c) -> p t c", t=6),
            in_=ap3(convw.tensor, 0, [[4, 128], [4 * 128, 6], [1, 4]])))
        DMA("sync", ch_ld, lambda e: e.dma_start(out=s_onorm[:], in_=onormw))
        wait("sync", ch_g)
        DMA("sync", ch_ld, lambda e: e.dma_start(out=s_mixbias[64:65, 0:1], in_=mixb))

        with ExitStack() as c2:
            s_mk = sbuf("s_mk", [128, 32 * 128], c=c2)
            s_up = sbuf("s_up", [128, 32 * 512], c=c2)
            ps_A0 = c2.enter_context(nc.psum_tensor("ps_A0", [128, 1024], F32))
            ps_B0 = c2.enter_context(nc.psum_tensor("ps_B0", [128, 1024], F32))
            DMA("gpsimd", ch_gld, lambda e: e.dma_start(
                out=s_mk[:].rearrange("p (g c) -> p g c", g=32),
                in_=ap3(mk_bd.tensor, 0, [[128, 128], [128 * 128, 32], [1, 128]])))
            DMA("gpsimd", ch_gld, lambda e: e.dma_start(
                out=s_up[:].rearrange("p (g c) -> p g c", g=32),
                in_=ap3(up_bd.tensor, 0, [[512, 128], [512 * 128, 32], [1, 512]])))
            wait("tensor", ch_gld, ch_g)
            for g_i in range(32):
                tgt = ps_A0 if (g_i % 2 == 0) else ps_B0
                T(lambda t, g_i=g_i, tgt=tgt: t.matmul(
                    tgt[:, 0:512], s_mk[:, 128 * g_i:128 * (g_i + 1)],
                    s_up[:, 512 * g_i:512 * (g_i + 1)], start=True, stop=True))
                wait("vector", ch_t)
                for j in range(4):
                    V(lambda v, g_i=g_i, j=j, tgt=tgt: v.tensor_copy(
                        s_compT[32 * j:32 * (j + 1), 128 * g_i:128 * (g_i + 1)],
                        tgt[32 * j:32 * (j + 1), 128 * j:128 * (j + 1)]))
                if g_i % 2 == 1:
                    wait("tensor", ch_v)
            wait("tensor", ch_v)
            wait("sync", ch_v)
            wait("scalar", ch_v)
            wait("gpsimd", ch_v)
            flush_events()

        with ExitStack() as c1:
            s_wot = sbuf("s_wot", [128, 8 * 1024], c=c1)
            s_wproj = sbuf("s_wproj", [128, 8 * 833], c=c1)
            DMA("gpsimd", ch_gld, lambda e: e.dma_start(
                out=s_wot[:].rearrange("p (t c) -> p t c", t=8),
                in_=ap3(wot.tensor, 0, [[1024, 128], [1024 * 128, 8], [1, 1024]])))
            DMA("gpsimd", ch_gld, lambda e: e.dma_start(
                out=s_wproj[:].rearrange("p (t c) -> p t c", t=8),
                in_=ap3(wproj.tensor, 0, [[833, 128], [833 * 128, 8], [1, 833]])))

            s_xs = sbuf("s_xs", [128, 8 * SEG], c=c1)
            s_lnx = sbuf("s_lnx", [128, 8 * SEG], c=c1)
            s_raw = sbuf("s_raw", [128, 6 * (SEG + 3)], c=c1)
            s_qkv = sbuf("s_qkv", [128, 6 * SEG], c=c1)
            s_khat = sbuf("s_khat", [128, 2 * SEG], c=c1)
            s_kb = sbuf("s_kb", [128, 2 * SEG], c=c1)
            s_vb = sbuf("s_vb", [128, 2 * SEG], c=c1)
            s_ktm = sbuf("s_ktm", [128, NCH * 256], c=c1)
            s_vtm = sbuf("s_vtm", [128, NCH * 256], c=c1)
            s_f_m1 = sbuf("s_f_m1", [128, SEG], F32, c=c1)
            s_f_r1 = sbuf("s_f_r1", [128, SEG], F32, c=c1)
            s_f_b = sbuf("s_f_b", [128, SEG], F32, c=c1)
            s_f_g = sbuf("s_f_g", [128, SEG], F32, c=c1)
            s_f_g1m = sbuf("s_f_g1m", [128, SEG], F32, c=c1)
            s_f_mix = sbuf("s_f_mix", [128, SEG], F32, c=c1)
            s_f_rq = sbuf("s_f_rq", [128, SEG], F32, c=c1)
            s_f_rk = sbuf("s_f_rk", [128, SEG], F32, c=c1)
            s_f_rms = sbuf("s_f_rms", [128, SEG], F32, c=c1)
            s_f_tmp = sbuf("s_f_tmp", [128, SEG], F32, c=c1)
            s_A = sbuf("s_A", [128, TW], c=c1)
            s_AT = sbuf("s_AT", [128, TW], c=c1)
            s_X = [sbuf(f"s_X{k}", [128, TW], c=c1) for k in range(6)]
            s_XT = [sbuf(f"s_XT{k}", [128, TW], c=c1) for k in range(2)]
            s_G = sbuf("s_G", [128, TW], c=c1)
            s_aT = sbuf("s_aT", [128, TW], c=c1)
            s_r = sbuf("s_r", [128, 256], c=c1)
            s_u = sbuf("s_u", [128, 256], c=c1)
            s_uema = sbuf("s_uema", [128, 2 * SEG], F32, c=c1)
            s_lcAt = sbuf("s_lcAt", [33, SEG], F32, c=c1)
            s_lcBt = sbuf("s_lcBt", [33, SEG], F32, c=c1)
            s_neglc = sbuf("s_neglc", [128, 4], F32, c=c1)
            s_f_lc = sbuf("s_f_lc", [128, SEG], F32, c=c1)
            s_f_p = sbuf("s_f_p", [128, SEG], F32, c=c1)
            s_uetm = sbuf("s_uetm", [128, NCH * 256], F32, c=c1)
            s_mt = sbuf("s_mt", [128, 128], F32, c=c1)
            s_mtf = sbuf("s_mtf", [128, NCH * 128], F32, c=c1)
            s_emtmp = sbuf("s_emtmp", [128, 128], F32, c=c1)
            s_ema = sbuf("s_ema", [128, 2 * SEG], F32, c=c1)
            s_oseg = sbuf("s_oseg", [128, 2 * SEG], F32, c=c1)
            s_obf = sbuf("s_obf", [128, 2 * SEG], c=c1)
            s_oag = sbuf("s_oag", [128, 8 * SEG], c=c1)
            s_ocat = sbuf("s_ocat", [128, 8 * QT], c=c1)
            s_attn = sbuf("s_attn", [128, 8 * QT], F32, c=c1)
            s_abf = sbuf("s_abf", [128, 8 * QT], c=c1)
            s_x1 = sbuf("s_x1", [128, 8 * QT], F32, c=c1)
            s_xres = sbuf("s_xres", [128, 8 * QT], F32, c=c1)
            s_fq_a = sbuf("s_fq_a", [128, QT], F32, c=c1)
            s_fq_b = sbuf("s_fq_b", [128, QT], F32, c=c1)
            s_fq_c = sbuf("s_fq_c", [128, QT], F32, c=c1)

            V(lambda v: v.memset(
                s_raw[:].rearrange("p (t c) -> p t c", t=6)[:, :, 0:3], 0.0))

            ps_A = c1.enter_context(nc.psum_tensor("ps_A", [128, 1024], F32))
            ps_B = c1.enter_context(nc.psum_tensor("ps_B", [128, 1024], F32))
            ps_C = c1.enter_context(nc.psum_tensor("ps_C", [128, 512], F32))
            ps_D = c1.enter_context(nc.psum_tensor("ps_D", [128, 512], F32))
            ps_E = c1.enter_context(nc.psum_tensor("ps_E", [128, 512], F32))
            ps_F = c1.enter_context(nc.psum_tensor("ps_F", [128, 512], F32))

            def row_store(seg, r0, src, nr=1, w=SEG):
                DMA("scalar", ch_sld, lambda e, seg=seg, r0=r0, src=src, nr=nr, w=w: e.dma_start(
                    out=ap3(rowbuf.tensor, (seg * 16 + r0) * SEG, [[SEG, nr], [1, w]]),
                    in_=src[0:nr, 0:w]))

            def row_bcast(seg, row, dst, w=SEG):
                DMA("scalar", ch_sld, lambda e, seg=seg, row=row, dst=dst, w=w: e.dma_start(
                    out=dst[:, 0:w],
                    in_=ap3(rowbuf.tensor, (seg * 16 + row) * SEG, [[0, 128], [1, w]])))

            def ln_fields(mf, vf, tmp, w=SEG):
                wait("vector", ch_t)
                V(lambda v: v.tensor_scalar(mf[:, 0:w], mf[:, 0:w], cst(C_I1024), None, ALU.mult))
                V(lambda v: v.tensor_scalar(vf[:, 0:w], vf[:, 0:w], cst(C_I1024), None, ALU.mult))
                wait("scalar", ch_v)
                S(lambda s: s.activation(tmp[:, 0:w], mf[:, 0:w], AF.Square))
                wait("vector", ch_s)
                V(lambda v: v.tensor_sub(vf[:, 0:w], vf[:, 0:w], tmp[:, 0:w]))
                wait("scalar", ch_v)
                S(lambda s: s.activation(vf[:, 0:w], vf[:, 0:w], AF.Ln, bias=cst(C_EPS_LN)))
                S(lambda s: s.activation(vf[:, 0:w], vf[:, 0:w], AF.Exp, scale=cst(C_NHALF)))

            ef_free = {}
            xload_done = {}

            xs_free = {}

            def emit_xload(seg):
                c0 = seg * SEG
                if seg == 0:
                    wait("gpsimd", ch_t, ch_s, ch_v)
                else:
                    tn, sn, vn = xs_free[seg - 1]
                    wait_at("gpsimd", ch_t, tn)
                    wait_at("gpsimd", ch_s, sn)
                    wait_at("gpsimd", ch_v, vn)
                DMA("gpsimd", ch_gld, lambda e, c0=c0: e.dma_start(
                    out=s_xs[:].rearrange("p (t c) -> p t c", t=8),
                    in_=ap3(x_in.tensor, 3 + c0, [[L + 3, 128], [(L + 3) * 128, 8], [1, SEG]])))
                xload_done[seg] = ch_gld.n

            def front(seg):
                wait("gpsimd", ch_t, ch_s, ch_v)
                wait("tensor", ch_v, ch_s, ch_sld, ch_ld, ch_gld, ch_g)
                wait("vector", ch_t, ch_s, ch_sld, ch_ld, ch_gld)
                wait("scalar", ch_t, ch_v, ch_sld, ch_ld, ch_gld)
                mm_done = [0, 0]
                for dt_i in range(8):
                    buf = s_sq if dt_i % 2 == 0 else s_sqB
                    wait_at("scalar", ch_t, mm_done[dt_i % 2])
                    S(lambda s, dt_i=dt_i, buf=buf: s.activation(
                        buf[:], s_xs[:, SEG * dt_i:SEG * (dt_i + 1)], AF.Square))
                    wait("tensor", ch_s)
                    T(lambda t, dt_i=dt_i, buf=buf: t.matmul(
                        ps_F[0:1, 0:SEG], s_ones[:], buf[:],
                        start=(dt_i == 0), stop=(dt_i == 7)))
                    mm_done[dt_i % 2] = ch_t.n
                for dt_i in range(8):
                    T(lambda t, dt_i=dt_i: t.matmul(ps_E[0:1, 0:SEG], s_ones[:],
                                                    s_xs[:, SEG * dt_i:SEG * (dt_i + 1)],
                                                    start=(dt_i == 0), stop=(dt_i == 7)))
                wait("vector", ch_t)
                V(lambda v: v.tensor_copy(s_row_a[0:1, :], ps_E[0:1, 0:SEG]))
                V(lambda v: v.tensor_copy(s_row_b[0:1, :], ps_F[0:1, 0:SEG]))
                wait("tensor", ch_v)
                T(lambda t: t.matmul(ps_E[:, 0:SEG], s_ones2[0:1, 0:128], s_row_a[0:1, 0:SEG],
                                     start=True, stop=True))
                wait("vector", ch_t)
                V(lambda v: v.tensor_copy(s_f_m1[:, 0:SEG], ps_E[:, 0:SEG]))
                wait("tensor", ch_v)
                T(lambda t: t.matmul(ps_F[:, 0:SEG], s_ones2[0:1, 0:128], s_row_b[0:1, 0:SEG],
                                     start=True, stop=True))
                wait("vector", ch_t)
                V(lambda v: v.tensor_copy(s_f_r1[:, 0:SEG], ps_F[:, 0:SEG]))
                wait("tensor", ch_v)
                ln_fields(s_f_m1, s_f_r1, s_f_tmp)
                wait("vector", ch_s)
                for dt_i in range(8):
                    V(lambda v, dt_i=dt_i: v.tensor_sub(
                        s_lnx[:, SEG * dt_i:SEG * (dt_i + 1)],
                        s_xs[:, SEG * dt_i:SEG * (dt_i + 1)], s_f_m1[:]))
                    V(lambda v, dt_i=dt_i: v.tensor_mul(
                        s_lnx[:, SEG * dt_i:SEG * (dt_i + 1)],
                        s_lnx[:, SEG * dt_i:SEG * (dt_i + 1)], s_f_r1[:]))

                if seg > 0:
                    V(lambda v: v.tensor_copy(
                        s_raw[:].rearrange("p (t c) -> p t c", t=6)[:, :, 0:3],
                        s_raw[:].rearrange("p (t c) -> p t c", t=6)[:, :, SEG:SEG + 3]))

                wait("tensor", ch_v)
                pevac = [0, 0]
                for ot in range(6):
                    tgt = ps_A if (ot % 2 == 0) else ps_B
                    wait_at("tensor", ch_v, pevac[ot % 2])
                    for kt in range(8):
                        T(lambda t, ot=ot, kt=kt, tgt=tgt: t.matmul(
                            tgt[:, 0:SEG],
                            s_wproj[:, 833 * kt + 128 * ot:833 * kt + 128 * (ot + 1)],
                            s_lnx[:, SEG * kt:SEG * (kt + 1)],
                            start=(kt == 0), stop=(kt == 7)))
                    wait("vector", ch_t)
                    V(lambda v, ot=ot, tgt=tgt: v.tensor_copy(
                        s_raw[:, (SEG + 3) * ot + 3:(SEG + 3) * (ot + 1)], tgt[:, 0:SEG]))
                    pevac[ot % 2] = ch_v.n
                for kt in range(8):
                    T(lambda t, kt=kt: t.matmul(
                        ps_E[0:65, 0:SEG], s_wproj[:, 833 * kt + 768:833 * kt + 833],
                        s_lnx[:, SEG * kt:SEG * (kt + 1)], start=(kt == 0), stop=(kt == 7)))
                wait("scalar", ch_t)
                S(lambda s: s.activation(s_row65[0:65, :], ps_E[0:65, 0:SEG], AF.Sigmoid,
                                         bias=s_mixbias[:]))
                wait("vector", ch_s)
                V(lambda v: v.tensor_copy(s_row65b[0:65, :], s_row65[0:65, :]))
                wait("tensor", ch_v)
                for p_, dstf in ((0, s_f_b), (32, s_f_g), (64, s_f_mix)):
                    T(lambda t, p_=p_: t.matmul(
                        ps_F[:, 0:SEG], s_ones2[p_:p_ + 1, 0:128], s_row65b[p_:p_ + 1, :],
                        start=True, stop=True, tile_position=(p_, 0) if p_ else None))
                    wait("vector", ch_t)
                    V(lambda v, dstf=dstf: v.tensor_copy(dstf[:], ps_F[:, 0:SEG]))
                    wait("tensor", ch_v)
                S(lambda s: s.activation(s_lcAt[32:33, :], s_row65[32:33, :], AF.Ln))
                wait("vector", ch_s)
                V(lambda v: v.tensor_scalar_max(s_lcAt[32:33, :], s_lcAt[32:33, :], -80.0))
                cur, oth = s_lcAt, s_lcBt
                base_off = 32 * SEG
                sh = 1
                while sh < CH:
                    V(lambda v, cur=cur, oth=oth, sh=sh: v.tensor_copy(
                        ap3(oth, base_off, [[SEG, 1], [CH, NCH], [1, sh]]),
                        ap3(cur, base_off, [[SEG, 1], [CH, NCH], [1, sh]])))
                    V(lambda v, cur=cur, oth=oth, sh=sh: v.tensor_add(
                        ap3(oth, base_off + sh, [[SEG, 1], [CH, NCH], [1, CH - sh]]),
                        ap3(cur, base_off + sh, [[SEG, 1], [CH, NCH], [1, CH - sh]]),
                        ap3(cur, base_off, [[SEG, 1], [CH, NCH], [1, CH - sh]])))
                    cur, oth = oth, cur
                    sh *= 2
                s_lc = cur
                wait("tensor", ch_v)
                for c in range(NCH):
                    T(lambda t, c=c, s_lc=s_lc: t.matmul(
                        ps_E[:, 1000 + c:1001 + c] if False else ps_F[:, c:c + 1],
                        s_lc[32:33, 128 * c:128 * (c + 1)], s_onef1[32:33, 0:1],
                        start=True, stop=True, tile_position=(32, 0)))
                wait("vector", ch_t)
                V(lambda v: v.tensor_scalar(s_neglc[:], ps_F[:, 0:4], -1.0, None, ALU.mult))
                wait("tensor", ch_v)
                T(lambda t, s_lc=s_lc: t.matmul(
                    ps_E[:, 0:SEG], s_ones2f[32:33, 0:128], s_lc[32:33, :],
                    start=True, stop=True, tile_position=(32, 0)))
                wait("vector", ch_t)
                V(lambda v: v.tensor_copy(s_f_lc[:], ps_E[:, 0:SEG]))
                wait("tensor", ch_v)

                wait("vector", ch_t)
                for ot in range(6):
                    rb = (SEG + 3) * ot
                    V(lambda v, ot=ot, rb=rb: v.tensor_scalar(
                        s_qkv[:, SEG * ot:SEG * (ot + 1)], s_raw[:, rb:rb + SEG],
                        s_convw[:, 4 * ot:4 * ot + 1], None, ALU.mult))
                    for j in (1, 2, 3):
                        V(lambda v, ot=ot, rb=rb, j=j: v.scalar_tensor_tensor(
                            s_qkv[:, SEG * ot:SEG * (ot + 1)],
                            s_raw[:, rb + j:rb + j + SEG],
                            s_convw[:, 4 * ot + j:4 * ot + j + 1],
                            s_qkv[:, SEG * ot:SEG * (ot + 1)], ALU.mult, ALU.add))
                    wait("scalar", ch_v)
                    S(lambda s, ot=ot: s.activation(
                        s_qkv[:, SEG * ot:SEG * (ot + 1)],
                        s_qkv[:, SEG * ot:SEG * (ot + 1)], AF.Silu))

                wait("scalar", ch_s)
                for base, row in ((0, 5), (2, 6)):
                    wait("scalar", ch_t)
                    for i in range(2):
                        buf = s_sq if i % 2 == 0 else s_sqB
                        S(lambda s, base=base, i=i, buf=buf: s.activation(
                            buf[:], s_qkv[:, SEG * (base + i):SEG * (base + i + 1)], AF.Square))
                        wait("tensor", ch_s)
                        T(lambda t, i=i, buf=buf: t.matmul(ps_F[0:1, 0:SEG], s_ones[:], buf[:],
                                                           start=(i == 0), stop=(i == 1)))
                    dst = s_row_a if row == 5 else s_row_b
                    wait("vector", ch_t)
                    V(lambda v, dst=dst: v.tensor_copy(dst[0:1, :], ps_F[0:1, 0:SEG]))
                    wait("tensor", ch_v)
                T(lambda t: t.matmul(ps_E[:, 0:SEG], s_ones2[0:1, 0:128], s_row_a[0:1, 0:SEG],
                                     start=True, stop=True))
                wait("vector", ch_t)
                V(lambda v: v.tensor_copy(s_f_rq[:, 0:SEG], ps_E[:, 0:SEG]))
                wait("tensor", ch_v)
                T(lambda t: t.matmul(ps_F[:, 0:SEG], s_ones2[0:1, 0:128], s_row_b[0:1, 0:SEG],
                                     start=True, stop=True))
                wait("vector", ch_t)
                V(lambda v: v.tensor_copy(s_f_rk[:, 0:SEG], ps_F[:, 0:SEG]))
                wait("tensor", ch_v)
                wait("scalar", ch_v)
                for f in (s_f_rq, s_f_rk):
                    S(lambda s, f=f: s.activation(f[:], f[:], AF.Ln, bias=cst(C_EPS_L2)))
                    S(lambda s, f=f: s.activation(f[:], f[:], AF.Exp, scale=cst(C_NHALF)))

                wait("vector", ch_s, ch_ld, ch_sld)
                for i in range(2):
                    V(lambda v, i=i: v.tensor_mul(
                        s_khat[:, SEG * i:SEG * (i + 1)],
                        s_qkv[:, SEG * (2 + i):SEG * (3 + i)], s_f_rk[:]))
                    V(lambda v, i=i: v.tensor_mul(
                        s_kb[:, SEG * i:SEG * (i + 1)],
                        s_khat[:, SEG * i:SEG * (i + 1)], s_f_b[:]))
                    V(lambda v, i=i: v.tensor_mul(
                        s_vb[:, SEG * i:SEG * (i + 1)],
                        s_qkv[:, SEG * (4 + i):SEG * (5 + i)], s_f_b[:]))
                xs_free[seg] = (ch_t.n, ch_s.n, ch_v.n)

            def back(seg):
                wait("sync", ch_v, ch_t)
                for (src, dst) in ((s_khat, s_ktm), (s_vb, s_vtm)):
                    for ft in range(2):
                        def tr(e, src=src, dst=dst, ft=ft):
                            o3 = ap3(dst, 128 * ft, [[NCH * 256, 128], [256, NCH], [1, 128]])
                            i3 = src[:, SEG * ft:SEG * (ft + 1)].rearrange("p (c t) -> p c t", c=NCH)
                            return e.dma_start_transpose(o3, i3)
                        DMA("sync", ch_ld, tr)

                if seg == DEBUG_SEG and dbg:
                    V(lambda v: v.tensor_copy(s_f_tmp[:, 0:2], s_ema_st[:]))
                V(lambda v: v.tensor_scalar(s_f_g1m[:], s_f_g[:], -1.0, 1.0, ALU.mult, ALU.add))
                for i in range(2):
                    V(lambda v, i=i: v.tensor_mul(
                        s_uema[:, SEG * i:SEG * (i + 1)],
                        s_qkv[:, SEG * (4 + i):SEG * (5 + i)], s_f_g1m[:]))
                wait("tensor", ch_v)
                for ft in range(2):
                    for c in range(NCH):
                        T(lambda t, ft=ft, c=c: t.transpose(
                            ps_A[:, 512 * ft + 128 * c:512 * ft + 128 * (c + 1)],
                            s_uema[:, SEG * ft + 128 * c:SEG * ft + 128 * (c + 1)],
                            s_identf[:]))
                wait("vector", ch_t)
                def ev_uetm(v):
                    o3 = ap3(s_uetm, 0, [[NCH * 256, 128], [128, 2 * NCH], [1, 128]])
                    i3 = ap3(ps_A, 0, [[1024, 128], [128, 2 * NCH], [1, 128]])
                    return v.tensor_copy(o3, i3)
                events["vector"].pop() if False else None
                def ev_uetm2(v):
                    for_cols = None
                    return v.tensor_copy(
                        ap3(s_uetm, 0, [[NCH * 256, 128], [1, 1024]]),
                        ap3(ps_A, 0, [[1024, 128], [1, 1024]]))
                V(ev_uetm2)
                wait("scalar", ch_v, ch_sld)
                S(lambda s: s.activation(s_f_p[:], s_f_lc[:], AF.Exp))
                for c in range(NCH):
                    wait("vector", ch_s)
                    V(lambda v, c=c: v.tensor_scalar(s_mt[:], s_f_lc[:, 128 * c:128 * (c + 1)],
                                                     s_neglc[:, c:c + 1], 0.0,
                                                     ALU.add, ALU.min))
                    wait("scalar", ch_v)
                    S(lambda s, c=c: s.activation(s_mt[:], s_mt[:], AF.Exp))
                    wait("vector", ch_s)
                    V(lambda v, c=c: v.tensor_tensor(s_mtf[:, 128 * c:128 * (c + 1)],
                                                     s_mt[:], s_mUinc[:], ALU.mult))
                wait("tensor", ch_v)
                for c in range(NCH):
                    for dv in range(2):
                        T(lambda t, c=c, dv=dv: t.matmul(
                            ps_B[:, 256 * c + 128 * dv:256 * c + 128 * (dv + 1)],
                            s_uetm[:, 512 * dv + 128 * c:512 * dv + 128 * (c + 1)],
                            s_mtf[:, 128 * c:128 * (c + 1)], start=True, stop=True))
                wait("vector", ch_t)
                for c in range(NCH):
                    for dv in range(2):
                        def ema_t1(v, c=c, dv=dv):
                            cb = ap3(s_ema_st, dv, [[2, 128], [0, 128]])
                            return v.tensor_tensor(s_emtmp[:],
                                                   s_f_p[:, 128 * c:128 * (c + 1)], cb, ALU.mult)
                        V(ema_t1)
                        def ema_t2(v, c=c, dv=dv):
                            return v.tensor_add(
                                s_ema[:, SEG * dv + 128 * c:SEG * dv + 128 * (c + 1)],
                                s_emtmp[:],
                                ps_B[:, 256 * c + 128 * dv:256 * c + 128 * (dv + 1)])
                        V(ema_t2)
                    for dv in range(2):
                        V(lambda v, c=c, dv=dv: v.tensor_copy(
                            s_ema_st[:, dv:dv + 1],
                            s_ema[:, SEG * dv + 128 * (c + 1) - 1:SEG * dv + 128 * (c + 1)]))
                wait("tensor", ch_v)

                wait("tensor", ch_ld, ch_v)
                for c in range(NCH):
                    for ki in range(2):
                        T(lambda t, c=c, ki=ki: t.matmul(
                            ps_A[:, 128 * c:128 * (c + 1)],
                            s_kb[:, SEG * ki + 128 * c:SEG * ki + 128 * (c + 1)],
                            s_khat[:, SEG * ki + 128 * c:SEG * ki + 128 * (c + 1)],
                            start=(ki == 0), stop=(ki == 1)))
                    for ki in range(2):
                        T(lambda t, c=c, ki=ki: t.matmul(
                            ps_B[:, 128 * c:128 * (c + 1)],
                            s_khat[:, SEG * ki + 128 * c:SEG * ki + 128 * (c + 1)],
                            s_kb[:, SEG * ki + 128 * c:SEG * ki + 128 * (c + 1)],
                            start=(ki == 0), stop=(ki == 1)))
                wait("vector", ch_t)
                mask_bc = lambda m: ap3(m, 0, [[128, 128], [0, NCH], [1, 128]])
                psv = lambda ps: ps[:, 0:TW].rearrange("p (c t) -> p c t", c=NCH)
                sbv = lambda s: s[:].rearrange("p (c t) -> p c t", c=NCH)
                V(lambda v: v.tensor_tensor(sbv(s_A), psv(ps_A), mask_bc(s_mLneg), ALU.mult))
                V(lambda v: v.tensor_tensor(sbv(s_AT), psv(ps_B), mask_bc(s_mUneg), ALU.mult))
                wait("tensor", ch_v)
                for c in range(NCH):
                    for ki in range(2):
                        T(lambda t, c=c, ki=ki: t.matmul(
                            ps_A[:, 128 * c:128 * (c + 1)],
                            s_khat[:, SEG * ki + 128 * c:SEG * ki + 128 * (c + 1)],
                            s_qkv[:, SEG * ki + 128 * c:SEG * ki + 128 * (c + 1)],
                            start=(ki == 0), stop=(ki == 1)))
                wait("vector", ch_t)
                V(lambda v: v.tensor_tensor(sbv(s_aT), psv(ps_A), mask_bc(s_mUinc), ALU.mult))
                wait("tensor", ch_v)

                curX, curXT = s_A, s_AT
                for lvl in range(6):
                    dstX = s_X[lvl]
                    dstXT = s_XT[lvl % 2] if lvl < 5 else None
                    for c in range(NCH):
                        T(lambda t, c=c, curX=curX, curXT=curXT: t.matmul(
                            ps_A[:, 128 * c:128 * (c + 1)],
                            curXT[:, 128 * c:128 * (c + 1)],
                            curX[:, 128 * c:128 * (c + 1)], start=True, stop=True))
                    if dstXT is not None:
                        for c in range(NCH):
                            T(lambda t, c=c, curX=curX, curXT=curXT: t.matmul(
                                ps_B[:, 128 * c:128 * (c + 1)],
                                curX[:, 128 * c:128 * (c + 1)],
                                curXT[:, 128 * c:128 * (c + 1)], start=True, stop=True))
                    wait("vector", ch_t)
                    V(lambda v, dstX=dstX: v.tensor_copy(dstX[:], ps_A[:, 0:TW]))
                    if dstXT is not None:
                        V(lambda v, dstXT=dstXT: v.tensor_copy(dstXT[:], ps_B[:, 0:TW]))
                    wait("tensor", ch_v)
                    curX, curXT = dstX, dstXT
                V(lambda v: v.tensor_tensor(sbv(s_G), sbv(s_AT),
                                            ap3(s_ident, 0, [[128, 128], [0, NCH], [1, 128]]),
                                            ALU.add))
                wait("tensor", ch_v)
                for lvl in range(6):
                    Xk = s_X[lvl]
                    for c in range(NCH):
                        T(lambda t, c=c, Xk=Xk: t.matmul(
                            ps_A[:, 128 * c:128 * (c + 1)], Xk[:, 128 * c:128 * (c + 1)],
                            s_G[:, 128 * c:128 * (c + 1)], start=True, stop=False))
                        T(lambda t, c=c: t.matmul(
                            ps_A[:, 128 * c:128 * (c + 1)], s_ident[:],
                            s_G[:, 128 * c:128 * (c + 1)], start=False, stop=True))
                    wait("vector", ch_t)
                    V(lambda v: v.tensor_copy(s_G[:], ps_A[:, 0:TW]))
                    wait("tensor", ch_v)

                for c in range(NCH):
                    T(lambda t, c=c: t.matmul(ps_C[:, 0:256], s_kb[:, 128 * c:128 * (c + 1)],
                                              s_Sbf[:, 0:256], start=True, stop=False))
                    T(lambda t, c=c: t.matmul(ps_C[:, 0:256],
                                              s_kb[:, SEG + 128 * c:SEG + 128 * (c + 1)],
                                              s_Sbf[:, 256:512], start=False, stop=True))
                    wait("vector", ch_t)
                    V(lambda v, c=c: v.tensor_sub(s_r[:], s_vtm[:, 256 * c:256 * (c + 1)],
                                                  ps_C[:, 0:256]))
                    wait("tensor", ch_v)
                    T(lambda t, c=c: t.matmul(ps_D[:, 0:256], s_G[:, 128 * c:128 * (c + 1)],
                                              s_r[:], start=True, stop=True))
                    wait("scalar", ch_t)
                    S(lambda s: s.copy(s_u[:], ps_D[:, 0:256]))
                    wait("tensor", ch_s)
                    for dv in range(2):
                        T(lambda t, c=c, dv=dv: t.matmul(
                            ps_D[:, 256 + 128 * dv:384 + 128 * dv],
                            s_Sbf[:, 128 * dv:128 * (dv + 1)],
                            s_qkv[:, 128 * c:128 * (c + 1)], start=True, stop=False))
                        T(lambda t, c=c, dv=dv: t.matmul(
                            ps_D[:, 256 + 128 * dv:384 + 128 * dv],
                            s_Sbf[:, 256 + 128 * dv:384 + 128 * dv],
                            s_qkv[:, SEG + 128 * c:SEG + 128 * (c + 1)],
                            start=False, stop=False))
                        T(lambda t, c=c, dv=dv: t.matmul(
                            ps_D[:, 256 + 128 * dv:384 + 128 * dv],
                            s_u[:, 128 * dv:128 * (dv + 1)],
                            s_aT[:, 128 * c:128 * (c + 1)], start=False, stop=True))
                    wait("vector", ch_t)
                    def ev_o(v, c=c):
                        o_ap = ap3(s_oseg, 128 * c, [[2 * SEG, 128], [SEG, 2], [1, 128]])
                        ps_view = ap3(ps_D, 256, [[512, 128], [128, 2], [1, 128]])
                        rq3 = ap3(s_f_rq, 128 * c, [[SEG, 128], [0, 2], [1, 128]])
                        return v.tensor_tensor(o_ap, ps_view, rq3, ALU.mult)
                    V(ev_o)
                    T(lambda t, c=c: t.matmul(ps_C[:, 0:256], s_ktm[:, 256 * c:256 * c + 128],
                                              s_u[:], start=True, stop=True))
                    T(lambda t, c=c: t.matmul(ps_C[:, 256:512],
                                              s_ktm[:, 256 * c + 128:256 * (c + 1)],
                                              s_u[:], start=True, stop=True))
                    wait("vector", ch_t)
                    V(lambda v: v.tensor_add(s_S[:], s_S[:], ps_C[:]))
                    V(lambda v: v.tensor_copy(s_Sbf[:], s_S[:]))
                    wait("tensor", ch_v)

                for i in range(2):
                    V(lambda v, i=i: v.tensor_sub(s_uema[:, SEG * i:SEG * (i + 1)],
                                                  s_ema[:, SEG * i:SEG * (i + 1)],
                                                  s_oseg[:, SEG * i:SEG * (i + 1)]))
                    V(lambda v, i=i: v.tensor_mul(s_uema[:, SEG * i:SEG * (i + 1)],
                                                  s_uema[:, SEG * i:SEG * (i + 1)], s_f_mix[:]))
                    V(lambda v, i=i: v.tensor_add(s_oseg[:, SEG * i:SEG * (i + 1)],
                                                  s_oseg[:, SEG * i:SEG * (i + 1)],
                                                  s_uema[:, SEG * i:SEG * (i + 1)]))
                wait("scalar", ch_v)
                wait("scalar", ch_t)
                for i in range(2):
                    buf = s_sq if i % 2 == 0 else s_sqB
                    S(lambda s, i=i, buf=buf: s.activation(buf[:], s_oseg[:, SEG * i:SEG * (i + 1)],
                                                           AF.Square))
                    wait("tensor", ch_s)
                    T(lambda t, i=i, buf=buf: t.matmul(ps_F[0:1, 0:SEG], s_ones[:], buf[:],
                                                       start=(i == 0), stop=(i == 1)))
                wait("vector", ch_t)
                V(lambda v: v.tensor_copy(s_row_a[0:1, :], ps_F[0:1, 0:SEG]))
                wait("tensor", ch_v)
                T(lambda t: t.matmul(ps_F[:, 0:SEG], s_ones2[0:1, 0:128], s_row_a[0:1, 0:SEG],
                                     start=True, stop=True))
                wait("vector", ch_t)
                V(lambda v: v.tensor_copy(s_f_rms[:, 0:SEG], ps_F[:, 0:SEG]))
                ef_free[seg] = ch_v.n
                wait("tensor", ch_v)
                wait("scalar", ch_v)
                S(lambda s: s.activation(s_f_rms[:], s_f_rms[:], AF.Ln,
                                         bias=cst(C_EPS_RMS), scale=cst(C_I256)))
                S(lambda s: s.activation(s_f_rms[:], s_f_rms[:], AF.Exp, scale=cst(C_NHALF)))
                wait("vector", ch_s, ch_gld)
                for i in range(2):
                    V(lambda v, i=i: v.tensor_mul(s_oseg[:, SEG * i:SEG * (i + 1)],
                                                  s_oseg[:, SEG * i:SEG * (i + 1)], s_f_rms[:]))
                    V(lambda v, i=i: v.tensor_scalar(s_obf[:, SEG * i:SEG * (i + 1)],
                                                     s_oseg[:, SEG * i:SEG * (i + 1)],
                                                     s_onorm[:, i:i + 1], None, ALU.mult))

                wait("gpsimd", ch_v)
                DMA("gpsimd", ch_gld, lambda e: e.dma_start(
                    out=ap3(cc_in.tensor, 0, [[SEG, 128], [128 * SEG, 2], [1, SEG]]),
                    in_=s_obf[:].rearrange("p (d t) -> p d t", d=2)))
                if DEBUG_ALL:
                    DMA("gpsimd", ch_gld, lambda e, seg=seg: e.dma_start(
                        out=ap3(dbgA["obf"].tensor, 1024 * seg,
                                [[NSEG * 1024, 128], [1, 1024]]),
                        in_=s_obf[:, 0:1024]))
                wait("gpsimd", ch_gld)
                CC(lambda g: g.collective_compute(
                    "AllGather", ALU.bypass,
                    replica_groups=[[0, 1, 2, 3], [4, 5, 6, 7]],
                    ins=[cc_in.opt()], outs=[cc_out.opt()]))
                wait("gpsimd", ch_cc)
                DMA("gpsimd", ch_gld, lambda e: e.dma_start(
                    out=s_oag[:].rearrange("p (f t) -> p f t", f=8),
                    in_=ap3(cc_out.tensor, 0, [[SEG, 128], [128 * SEG, 8], [1, SEG]])))

            def p15(seg):
                wait("tensor", ch_v, ch_s)
                wait("vector", ch_gld, ch_ld)
                V(lambda v: v.tensor_scalar(
                    s_ocat[:].rearrange("p (f t) -> p f t", f=8),
                    ap3(s_oag, 0, [[8 * SEG, 128], [SEG, 8], [1, QT]]),
                    s_msel[:, 0:1], None, ALU.mult))
                for p in range(1, 4):
                    V(lambda v, p=p: v.scalar_tensor_tensor(
                        s_ocat[:].rearrange("p (f t) -> p f t", f=8),
                        ap3(s_oag, QT * p, [[8 * SEG, 128], [SEG, 8], [1, QT]]),
                        s_msel[:, p:p + 1],
                        s_ocat[:].rearrange("p (f t) -> p f t", f=8),
                        ALU.mult, ALU.add))
                if DEBUG_ALL:
                    wait("gpsimd", ch_v)
                    DMA("gpsimd", ch_gld, lambda e, seg=seg: e.dma_start(
                        out=ap3(dbgA["oag"].tensor, 4096 * seg,
                                [[NSEG * 4096, 128], [1, 4096]]),
                        in_=s_oag[:, 0:4096]))
                    DMA("gpsimd", ch_gld, lambda e, seg=seg: e.dma_start(
                        out=ap3(dbgA["ocat"].tensor, 1024 * seg,
                                [[NSEG * 1024, 128], [1, 1024]]),
                        in_=s_ocat[:, 0:1024]))
                wait("tensor", ch_v)
                pevac = [0, 0]
                for ot in range(8):
                    tgt = ps_C if (ot % 2 == 0) else ps_D
                    wait_at("tensor", ch_v, pevac[ot % 2])
                    for kt in range(8):
                        T(lambda t, ot=ot, kt=kt, tgt=tgt: t.matmul(
                            tgt[:, 0:QT],
                            s_wot[:, 1024 * kt + 128 * ot:1024 * kt + 128 * (ot + 1)],
                            s_ocat[:, QT * kt:QT * (kt + 1)],
                            start=(kt == 0), stop=(kt == 7)))
                    wait("vector", ch_t)
                    V(lambda v, ot=ot, tgt=tgt: v.tensor_copy(
                        s_attn[:, QT * ot:QT * (ot + 1)], tgt[:, 0:QT]))
                    pevac[ot % 2] = ch_v.n
                V(lambda v: v.tensor_copy(s_abf[:], s_attn[:]))
                wait("tensor", ch_v)
                for ot in range(8):
                    T(lambda t, ot=ot: t.matmul(ps_C[0:1, 0:QT], s_ones[:],
                                                s_abf[:, QT * ot:QT * (ot + 1)],
                                                start=(ot == 0), stop=(ot == 7)))
                wait("scalar", ch_v)
                mm_done = [0, 0]
                for ot in range(8):
                    buf = s_sqC if ot % 2 == 0 else s_sqD
                    wait_at("scalar", ch_t, mm_done[ot % 2])
                    S(lambda s, ot=ot, buf=buf: s.activation(
                        buf[:, 0:QT], s_abf[:, QT * ot:QT * (ot + 1)], AF.Square))
                    wait("tensor", ch_s)
                    T(lambda t, ot=ot, buf=buf: t.matmul(
                        ps_D[0:1, 0:QT], s_ones[:], buf[:, 0:QT],
                        start=(ot == 0), stop=(ot == 7)))
                    mm_done[ot % 2] = ch_t.n
                wait("vector", ch_t)
                V(lambda v: v.tensor_copy(s_row_c[0:1, 0:QT], ps_C[0:1, 0:QT]))
                V(lambda v: v.tensor_copy(s_row_d[0:1, 0:QT], ps_D[0:1, 0:QT]))
                wait("tensor", ch_v)
                T(lambda t: t.matmul(ps_C[:, 0:QT], s_ones2[0:1, 0:128], s_row_c[0:1, 0:QT],
                                     start=True, stop=True))
                wait("vector", ch_t)
                V(lambda v: v.tensor_copy(s_fq_a[:, 0:QT], ps_C[:, 0:QT]))
                wait("tensor", ch_v)
                T(lambda t: t.matmul(ps_D[:, 0:QT], s_ones2[0:1, 0:128], s_row_d[0:1, 0:QT],
                                     start=True, stop=True))
                wait("vector", ch_t)
                V(lambda v: v.tensor_copy(s_fq_b[:, 0:QT], ps_D[:, 0:QT]))
                wait("tensor", ch_v)
                ln_fields(s_fq_a, s_fq_b, s_fq_c, w=QT)
                if seg == DEBUG_SEG and dbg:
                    wait("vector", ch_s)
                    V(lambda v: v.tensor_copy(s_f_g1m[:, 0:QT], s_fq_a[:, 0:QT]))
                    V(lambda v: v.tensor_copy(s_f_g1m[:, QT:2 * QT], s_fq_b[:, 0:QT]))
                DMA("sync", ch_ld, lambda e, seg=seg: e.dma_start(
                    out=s_xres[:].rearrange("p (f t) -> p f t", f=8),
                    in_=ap3(x_mine.tensor, QT * seg, [[MYT, 128], [MYT * 128, 8], [1, QT]])))
                wait("vector", ch_s, ch_ld)
                for ot in range(8):
                    V(lambda v, ot=ot: v.tensor_sub(s_x1[:, QT * ot:QT * (ot + 1)],
                                                    s_attn[:, QT * ot:QT * (ot + 1)], s_fq_a[:]))
                    V(lambda v, ot=ot: v.tensor_mul(s_x1[:, QT * ot:QT * (ot + 1)],
                                                    s_x1[:, QT * ot:QT * (ot + 1)], s_fq_b[:]))
                    V(lambda v, ot=ot: v.tensor_add(s_x1[:, QT * ot:QT * (ot + 1)],
                                                    s_x1[:, QT * ot:QT * (ot + 1)],
                                                    s_xres[:, QT * ot:QT * (ot + 1)]))
                wait("sync", ch_v)
                DMA("sync", ch_ld, lambda e, seg=seg: e.dma_start(
                    out=ap3(x1_dram.tensor, QT * seg, [[MYT, 128], [MYT * 128, 8], [1, QT]]),
                    in_=s_x1[:].rearrange("p (f t) -> p f t", f=8)))
                V(lambda v: v.tensor_copy(s_abf[:], s_x1[:]))
                wait("tensor", ch_v)
                for ot in range(8):
                    T(lambda t, ot=ot: t.matmul(ps_C[0:1, 0:QT], s_ones[:],
                                                s_abf[:, QT * ot:QT * (ot + 1)],
                                                start=(ot == 0), stop=(ot == 7)))
                wait("scalar", ch_v)
                mm_done = [0, 0]
                for ot in range(8):
                    buf = s_sqC if ot % 2 == 0 else s_sqD
                    wait_at("scalar", ch_t, mm_done[ot % 2])
                    S(lambda s, ot=ot, buf=buf: s.activation(
                        buf[:, 0:QT], s_abf[:, QT * ot:QT * (ot + 1)], AF.Square))
                    wait("tensor", ch_s)
                    T(lambda t, ot=ot, buf=buf: t.matmul(
                        ps_D[0:1, 0:QT], s_ones[:], buf[:, 0:QT],
                        start=(ot == 0), stop=(ot == 7)))
                    mm_done[ot % 2] = ch_t.n
                wait("vector", ch_t)
                V(lambda v: v.tensor_copy(s_row_c[0:1, 0:QT], ps_C[0:1, 0:QT]))
                V(lambda v: v.tensor_copy(s_row_d[0:1, 0:QT], ps_D[0:1, 0:QT]))
                wait("tensor", ch_v)
                T(lambda t: t.matmul(ps_C[:, 0:QT], s_ones2[0:1, 0:128], s_row_c[0:1, 0:QT],
                                     start=True, stop=True))
                wait("vector", ch_t)
                V(lambda v: v.tensor_copy(s_fq_a[:, 0:QT], ps_C[:, 0:QT]))
                wait("tensor", ch_v)
                T(lambda t: t.matmul(ps_D[:, 0:QT], s_ones2[0:1, 0:128], s_row_d[0:1, 0:QT],
                                     start=True, stop=True))
                wait("vector", ch_t)
                V(lambda v: v.tensor_copy(s_fq_b[:, 0:QT], ps_D[:, 0:QT]))
                wait("tensor", ch_v)
                ln_fields(s_fq_a, s_fq_b, s_fq_c, w=QT)
                wait("vector", ch_s, ch_ld)
                for ot in range(8):
                    V(lambda v, ot=ot: v.tensor_sub(s_x1[:, QT * ot:QT * (ot + 1)],
                                                    s_x1[:, QT * ot:QT * (ot + 1)], s_fq_a[:]))
                    V(lambda v, ot=ot: v.tensor_mul(s_abf[:, QT * ot:QT * (ot + 1)],
                                                    s_x1[:, QT * ot:QT * (ot + 1)], s_fq_b[:]))
                wait("sync", ch_v)
                DMA("sync", ch_ld, lambda e, seg=seg: e.dma_start(
                    out=ap3(ln4_dram.tensor, QT * seg, [[MYT, 128], [MYT * 128, 8], [1, QT]]),
                    in_=s_abf[:].rearrange("p (f t) -> p f t", f=8)))
                if seg == DEBUG_SEG:
                    wait("gpsimd", ch_t, ch_v, ch_s, ch_ld, ch_sld)
                    for nm, src, w in (("obf", s_obf, 2 * SEG), ("ocat", s_ocat, 8 * QT),
                                       ("attn", s_attn, 8 * QT), ("x1", s_x1, 8 * QT),
                                       ("G", s_G, TW), ("vtm", s_vtm, NCH * 256),
                                       ("ktm", s_ktm, NCH * 256), ("oag", s_oag, 8 * SEG),
                                       ("qkv", s_qkv, 6 * SEG), ("lnx", s_lnx, 8 * SEG),
                                       ("ema", s_ema, 2 * SEG), ("frq", s_f_rq, SEG),
                                       ("aT", s_aT, TW), ("emin", s_f_tmp, 2),
                                       ("ln2f", s_f_g1m, 2 * QT),
                                       ("uem", s_uema, 2 * SEG), ("fg", s_f_g, SEG)):
                        DMA("gpsimd", ch_gld, lambda e, nm=nm, src=src, w=w: e.dma_start(
                            out=ap3(dbg[nm].tensor, 0, [[8 * SEG, 128], [1, w]]),
                            in_=src[:, 0:w]))
                    wait("gpsimd", ch_gld)

            emit_xload(0)
            for seg in range(NSEG):
                front(seg)
                back(seg)
                p15(seg)
                if seg + 1 < NSEG:
                    emit_xload(seg + 1)

            wait("tensor")
            wait("vector")
            wait("scalar")
            wait("sync")
            wait("gpsimd")
            flush_events()

        with ExitStack() as c4:
            s_ln4 = sbuf("s_ln4", [128, 4 * PBT], c=c4)
            s_z = sbuf("s_z", [128, 16 * PBT], c=c4)
            s_wc = sbuf("s_wc", [128, 16 * 1024], c=c4)
            s_wcB = sbuf("s_wcB", [128, 16 * 1024], c=c4)
            s_acc = sbuf("s_acc", [128, 8 * MYT], F32, c=c4)
            s_sq2 = sbuf("s_sq2", [128, 512], c=c4)
            s_fb_a = sbuf("s_fb_a", [128, PBT], F32, c=c4)
            s_fb_b = sbuf("s_fb_b", [128, PBT], F32, c=c4)
            s_x2 = sbuf("s_x2", [128, PBT], F32, c=c4)
            s_xt1 = sbuf("s_xt1", [128, PBT], F32, c=c4)
            s_fb_c = s_x2

            ps_z0 = c4.enter_context(nc.psum_tensor("ps_z0", [128, 1024], F32))
            ps_z1 = c4.enter_context(nc.psum_tensor("ps_z1", [128, 1024], F32))
            ps_c0 = c4.enter_context(nc.psum_tensor("ps_c0", [128, 512], F32))
            ps_c1 = c4.enter_context(nc.psum_tensor("ps_c1", [128, 512], F32))
            ps_r = c4.enter_context(nc.psum_tensor("ps_r", [128, 512], F32))

            ln4v = s_ln4[:].rearrange("p (f t) -> p f t", f=4)

            wc_bufs = [s_wc, s_wcB]
            wc_done = {}

            def emit_wcload(e8):
                buf = wc_bufs[e8 % 2]
                wait("gpsimd", ch_t, ch_v, ch_s)
                DMA("gpsimd", ch_gld, lambda e, e8=e8, buf=buf: e.dma_start(
                    out=buf[:].rearrange("p (k c) -> p k c", k=16),
                    in_=ap3(wct.tensor, 2048 * 1024 * e8,
                            [[1024, 128], [1024 * 128, 16], [1, 1024]])))
                wc_done[e8] = ch_gld.n

            emit_wcload(0)
            for e8 in range(8):
                wcb = wc_bufs[e8 % 2]
                if e8 + 1 < 8:
                    emit_wcload(e8 + 1)
                wait("tensor", ch_v, ch_s)
                for bi in range(2):
                    tb0 = PBT * bi
                    wait("sync", ch_t)
                    DMA("sync", ch_ld, lambda e, tb0=tb0, e8=e8: e.dma_start(
                        out=ln4v,
                        in_=ap3(ln4_dram.tensor, tb0 + MYT * 128 * 4 * (e8 % 2),
                                [[MYT, 128], [MYT * 128, 4], [1, PBT]])))
                    wait("tensor", ch_ld, ch_v)
                    gelu_done = [0, 0]
                    for zi in range(16):
                        zt = 16 * e8 + zi
                        bp = 32 * ((zt % 32) % 4)
                        ft = zi // 4
                        tgt = ps_z0 if (zi % 2 == 0) else ps_z1
                        wait_at("tensor", ch_s, gelu_done[zi % 2])
                        for th in range(2):
                            T(lambda t, zt=zt, bp=bp, ft=ft, th=th, tgt=tgt, tb0=tb0: t.matmul(
                                tgt[:, 512 * th:512 * (th + 1)],
                                s_compT[bp:bp + 32,
                                        128 * ((zt % 32) // 4) + 1024 * (zt // 32):
                                        128 * ((zt % 32) // 4) + 1024 * (zt // 32) + 128],
                                ln4v[bp:bp + 32, ft, 512 * th:512 * (th + 1)],
                                start=True, stop=True, tile_position=(bp, 0)))
                        wait("scalar", ch_t)
                        S(lambda s, zi=zi, tgt=tgt: s.activation(
                            s_z[:, PBT * zi:PBT * (zi + 1)], tgt[:, 0:PBT], AF.Gelu))
                        gelu_done[zi % 2] = ch_s.n
                    wait("tensor", ch_s)
                    wait_at("tensor", ch_gld, wc_done[e8])
                    evac_done = [0, 0]
                    for ot in range(8):
                        for th in range(2):
                            par = (ot * 2 + th) % 2
                            tgt = ps_c0 if par == 0 else ps_c1
                            wait_at("tensor", ch_v, evac_done[par])
                            for kt in range(16):
                                T(lambda t, ot=ot, th=th, kt=kt, tgt=tgt, wcb=wcb: t.matmul(
                                    tgt[:, 0:512],
                                    wcb[:, 1024 * kt + 128 * ot:1024 * kt + 128 * (ot + 1)],
                                    s_z[:, PBT * kt + 512 * th:PBT * kt + 512 * (th + 1)],
                                    start=(kt == 0), stop=(kt == 15)))
                            wait("vector", ch_t)
                            col = MYT * ot + tb0 + 512 * th
                            if e8 == 0:
                                V(lambda v, col=col, tgt=tgt: v.tensor_copy(
                                    s_acc[:, col:col + 512], tgt[:, 0:512]))
                            else:
                                V(lambda v, col=col, tgt=tgt: v.tensor_add(
                                    s_acc[:, col:col + 512], s_acc[:, col:col + 512],
                                    tgt[:, 0:512]))
                            evac_done[par] = ch_v.n
                    wait("tensor", ch_v)


            for bi in range(2):
                tb0 = PBT * bi
                for th in range(2):
                    for ot in range(8):
                        col = MYT * ot + tb0 + 512 * th
                        wait("vector", ch_t, ch_s)
                        V(lambda v, col=col: v.tensor_copy(s_sq2[:], s_acc[:, col:col + 512]))
                        wait("scalar", ch_v)
                        S(lambda s: s.activation(s_sq[:, 0:512], s_sq2[:], AF.Square))
                        wait("tensor", ch_v, ch_s)
                        T(lambda t, ot=ot: t.matmul(ps_r[0:1, 0:512], s_ones[:], s_sq2[:],
                                                    start=(ot == 0), stop=(ot == 7)))
                        T(lambda t, ot=ot: t.matmul(ps_c0[0:1, 0:512], s_ones[:], s_sq[:, 0:512],
                                                    start=(ot == 0), stop=(ot == 7)))
                    wait("vector", ch_t)
                    V(lambda v: v.tensor_copy(s_row_a[0:1, 0:512], ps_r[0:1, 0:512]))
                    V(lambda v: v.tensor_copy(s_row_b[0:1, 0:512], ps_c0[0:1, 0:512]))
                    wait("tensor", ch_v)
                    T(lambda t, th=th: t.matmul(ps_r[:, 0:512], s_ones2[0:1, 0:128],
                                                s_row_a[0:1, 0:512], start=True, stop=True))
                    T(lambda t, th=th: t.matmul(ps_c0[:, 0:512], s_ones2[0:1, 0:128],
                                                s_row_b[0:1, 0:512], start=True, stop=True))
                    wait("vector", ch_t)
                    V(lambda v, th=th: v.tensor_copy(s_fb_a[:, 512 * th:512 * (th + 1)],
                                                     ps_r[:, 0:512]))
                    V(lambda v, th=th: v.tensor_copy(s_fb_b[:, 512 * th:512 * (th + 1)],
                                                     ps_c0[:, 0:512]))
                    wait("tensor", ch_v)
                wait("vector", ch_t)
                V(lambda v: v.tensor_scalar(s_fb_a[:], s_fb_a[:], cst(C_I1024), None, ALU.mult))
                V(lambda v: v.tensor_scalar(s_fb_b[:], s_fb_b[:], cst(C_I1024), None, ALU.mult))
                wait("scalar", ch_v, ch_ld)
                S(lambda s: s.activation(s_fb_c[:], s_fb_a[:], AF.Square))
                wait("vector", ch_s)
                V(lambda v: v.tensor_sub(s_fb_b[:], s_fb_b[:], s_fb_c[:]))
                wait("scalar", ch_v)
                S(lambda s: s.activation(s_fb_b[:], s_fb_b[:], AF.Ln, bias=cst(C_EPS_LN)))
                S(lambda s: s.activation(s_fb_b[:], s_fb_b[:], AF.Exp, scale=cst(C_NHALF)))
                for ot in range(8):
                    col = MYT * ot + tb0
                    wait("sync", ch_v)
                    DMA("sync", ch_ld, lambda e, ot=ot, tb0=tb0: e.dma_start(
                        out=s_xt1[:],
                        in_=ap3(x1_dram.tensor, MYT * 128 * ot + tb0, [[MYT, 128], [1, PBT]])))
                    wait("vector", ch_s, ch_ld)
                    V(lambda v, col=col: v.tensor_sub(s_x2[:], s_acc[:, col:col + PBT], s_fb_a[:]))
                    V(lambda v: v.tensor_mul(s_x2[:], s_x2[:], s_fb_b[:]))
                    V(lambda v: v.tensor_add(s_x2[:], s_x2[:], s_xt1[:]))
                    wait("sync", ch_v)
                    DMA("sync", ch_ld, lambda e, ot=ot, tb0=tb0: e.dma_start(
                        out=ap3(out.tensor, MYT * 128 * ot + tb0, [[MYT, 128], [1, PBT]]),
                        in_=s_x2[:]))
            wait("sync")
            wait("gpsimd")
            flush_events()

    return nc


_NC = None
_LAST = None


def _get_nc():
    global _NC
    if _NC is None:
        _NC = build_program()
    return _NC


def kernel(**inputs):
    f32 = np.float32
    hs = np.asarray(inputs["L0_attn_hidden_states"], f32)
    Wq = np.asarray(inputs["L0_attn_Wq"], f32)
    Wk = np.asarray(inputs["L0_attn_Wk"], f32)
    Wv = np.asarray(inputs["L0_attn_Wv"], f32)
    Wb = np.asarray(inputs["L0_attn_Wb"], f32)
    Wdec = np.asarray(inputs["L0_attn_Wdec"], f32)
    Wmix = np.asarray(inputs["L0_attn_Wmix"], f32)
    mix_bias = np.asarray(inputs["L0_attn_mix_bias"], f32)
    wq_conv = np.asarray(inputs["L0_attn_wq_conv"], f32)
    wk_conv = np.asarray(inputs["L0_attn_wk_conv"], f32)
    wv_conv = np.asarray(inputs["L0_attn_wv_conv"], f32)
    onorm_w = np.asarray(inputs["L0_attn_onorm_w"], f32)
    Wo = np.asarray(inputs["L0_attn_Wo"], f32)
    maker = np.asarray(inputs["L0_ffn_rail_maker"], f32)
    up = np.asarray(inputs["L0_ffn_rail_up_proj"], f32)
    Wc = np.asarray(inputs["L0_ffn_w_cproj"], f32)

    consts = np.tile(np.array([1e-5, 1e-6, 1e-5, -0.5, 1.0 / 1024.0, 1.0 / 256.0, 1.0, 0.0],
                              f32), (128, 1))
    wct = np.ascontiguousarray(Wc.T)
    wot = np.ascontiguousarray(Wo.T)

    mk_bd = np.zeros((32, 128, 128), f32)
    up_bd = np.zeros((32, 128, 512), f32)
    for g in range(32):
        hh, rg = g // 8, g % 8
        for rr in range(8):
            r = 8 * rg + rr
            mk_bd[g, 16 * rr:16 * rr + 16, 16 * rr:16 * rr + 16] = maker[hh, r].T
            up_bd[g, 16 * rr:16 * rr + 16, 64 * rr:64 * rr + 64] = up[hh, r].T

    x_pad = [None, None]
    for b in range(2):
        xt = np.ascontiguousarray(hs[b].T)
        x_pad[b] = np.ascontiguousarray(
            np.concatenate([np.zeros((D, 3), f32), xt], axis=1))

    in_maps = []
    for c in range(8):
        b, h = c // 4, c % 4
        xt = x_pad[b][:, 3:]
        x_mine = np.ascontiguousarray(np.concatenate(
            [xt[:, SEG * s + QT * h:SEG * s + QT * h + QT] for s in range(NSEG)], axis=1))
        sl = slice(DK * h, DK * (h + 1))
        bgm = np.zeros((D, 65), f32)
        bgm[:, 0] = Wb.T[:, h]
        bgm[:, 32] = Wdec.T[:, h]
        bgm[:, 64] = Wmix.T[:, h]
        wp = np.ascontiguousarray(np.concatenate(
            [Wq.T[:, sl], Wk.T[:, sl], Wv.T[:, sl], bgm], axis=1))
        cw = np.ascontiguousarray(np.concatenate(
            [wq_conv[sl], wk_conv[sl], wv_conv[sl]], axis=0))
        ms = np.zeros((128, 4), f32)
        ms[:, h] = 1.0
        in_maps.append({
            "x_in": x_pad[b],
            "x_mine": x_mine,
            "wproj": wp,
            "convw": cw,
            "wot": wot,
            "wct": wct,
            "mk_bd": mk_bd,
            "up_bd": up_bd,
            "onormw": np.ascontiguousarray(onorm_w.reshape(2, 128).T),
            "mixb": mix_bias[h].reshape(1, 1).astype(f32),
            "consts": consts,
            "msel": ms,
        })

    nc = _get_nc()
    res = run_bass_kernel_spmd(nc, in_maps, core_ids=list(range(8)))
    global _LAST
    _LAST = res

    out_full = np.empty((2, L, D), f32)
    for c in range(8):
        b, h = c // 4, c % 4
        sh = res.results[c]["out"]
        sh = np.asarray(sh, f32).T.reshape(NSEG, QT, D)
        out_full[b].reshape(NSEG, 4, QT, D)[:, h] = sh
    return out_full
